# version 15
# baseline (speedup 1.0000x reference)
import math
from collections import defaultdict
from contextlib import ExitStack

import numpy as np

import concourse.bass as bass
import concourse.mybir as mybir
from concourse.bass_utils import run_bass_kernel_spmd
from concourse.aot_env import allow_ici_all_to_all

F32 = mybir.dt.float32
BF16 = mybir.dt.bfloat16
AF = mybir.ActivationFunctionType
ALU = mybir.AluOpType
AX = mybir.AxisListType.X

S = 8192
D = 2048
NCORE = 8
TOK = S // NCORE
NT = S // 128
NB = S // 512
NTO = TOK // 128
EI = 1408
ISH = 2816
SCL = 1.0 / math.sqrt(128.0)
EPS = 1e-5


_DEBUG = False


class EV:
    def __init__(self):
        self.events = []

    def add(self, eng, fn, waits=(), inc=None):
        self.events.append((eng, fn, tuple(waits), inc))


def _build():
    nc = bass.Bass("TRN2", target_bir_lowering=False)

    x_full = nc.dram_tensor("x_full", [S, D], F32, kind="ExternalInput")
    x_own = nc.dram_tensor("x_own", [TOK, D], F32, kind="ExternalInput")
    x_t = nc.dram_tensor("x_t", [D, S], F32, kind="ExternalInput")
    wq_s = nc.dram_tensor("wq_s", [D, 256], F32, kind="ExternalInput")
    wk_s = nc.dram_tensor("wk_s", [D, 256], F32, kind="ExternalInput")
    wv_s = nc.dram_tensor("wv_s", [D, 256], F32, kind="ExternalInput")
    wo_full = nc.dram_tensor("wo_full", [D, D], F32, kind="ExternalInput")
    c2_t = nc.dram_tensor("c2_t", [S, 256], F32, kind="ExternalInput")
    s2_t = nc.dram_tensor("s2_t", [S, 256], F32, kind="ExternalInput")
    ident_in = nc.dram_tensor("ident_in", [128, 128], F32, kind="ExternalInput")
    mask_in = nc.dram_tensor("mask_in", [128, 4 * 512], F32, kind="ExternalInput")
    gate_wTd = nc.dram_tensor("gate_wTd", [D, 8], F32, kind="ExternalInput")
    eg_w = nc.dram_tensor("eg_w", [8, D, EI], F32, kind="ExternalInput")
    eu_w = nc.dram_tensor("eu_w", [8, D, EI], F32, kind="ExternalInput")
    ed_w = nc.dram_tensor("ed_w", [8, EI, D], F32, kind="ExternalInput")
    sg_w = nc.dram_tensor("sg_w", [D, ISH], F32, kind="ExternalInput")
    su_w = nc.dram_tensor("su_w", [D, ISH], F32, kind="ExternalInput")
    sd_w = nc.dram_tensor("sd_w", [ISH, D], F32, kind="ExternalInput")
    out_d = nc.dram_tensor("out", [TOK, D], F32, kind="ExternalOutput")

    a2a_in = nc.dram_tensor("a2a_in", [D, TOK], BF16, kind="Internal")
    a2a_out = nc.dram_tensor("a2a_out", [D, TOK], BF16, kind="Internal")
    if _DEBUG:
        dbg_qt = nc.dram_tensor("dbg_qt", [128, 2 * S], BF16, kind="ExternalOutput")
        dbg_kt = nc.dram_tensor("dbg_kt", [128, 2 * S], BF16, kind="ExternalOutput")
        dbg_vb = nc.dram_tensor("dbg_vb", [128, NT * 256], BF16, kind="ExternalOutput")
        dbg_bias = nc.dram_tensor("dbg_bias", [128, 4], F32, kind="ExternalOutput")
        dbg_at = nc.dram_tensor("dbg_at", [128, 2 * S], BF16, kind="ExternalOutput")
        dbg_a2a = nc.dram_tensor("dbg_a2a", [D, TOK], BF16, kind="ExternalOutput")
        dbg_cmb = nc.dram_tensor("dbg_cmb", [128, NTO * 8], F32, kind="ExternalOutput")
        dbg_xnto = nc.dram_tensor("dbg_xnto", [128, 16 * TOK], BF16, kind="ExternalOutput")
        dbg_act = nc.dram_tensor("dbg_act", [128, 11 * TOK], BF16, kind="ExternalOutput")

    root = ExitStack()
    def sbt(st, n, s, d, side=None):
        return st.enter_context(nc.sbuf_tensor(n, s, d, side=side))
    def pst(st, n, s, d=F32):
        return st.enter_context(nc.psum_tensor(n, s, d))

    identb = sbt(root, "identb", [128, 128], BF16)
    identf = sbt(root, "identf", [128, 128], F32)
    onesb = sbt(root, "onesb", [128, 1], BF16)
    ones1b = sbt(root, "ones1b", [1, 128], BF16)
    onesf1 = sbt(root, "onesf1", [1, 1], F32)
    bias_sb = sbt(root, "bias_sb", [128, 1], F32)
    mq2 = sbt(root, "mq2", [1, 2], F32)
    nsqb = sbt(root, "nsqb", [128, 6], F32)
    mqr = sbt(root, "mqr", [128, 2], F32)
    mqrb = sbt(root, "mqrb", [128, 2], BF16)
    sc1 = sbt(root, "sc1", [1, 6], F32)
    sc1b = sbt(root, "sc1b", [1, 1], BF16)
    xf = [sbt(root, f"xf{i}", [128, D], F32) for i in range(2)]
    xn = [sbt(root, f"xn{i}", [128, D], BF16) for i in range(2)]
    dump = sbt(root, "dump", [128, D], BF16)
    srow = [sbt(root, f"srow{i}", [128, 1], F32) for i in range(2)]
    ssq = [sbt(root, f"ssq{i}", [128, 1], F32) for i in range(2)]
    meanb = [sbt(root, f"meanb{i}", [128, 1], F32) for i in range(2)]
    m2b = sbt(root, "m2b", [128, 1], F32)
    vepsb = sbt(root, "vepsb", [128, 1], F32)
    rstd = [sbt(root, f"rstd{i}", [128, 1], F32) for i in range(2)]
    lnvb = [sbt(root, f"lnvb{i}", [128, 1], F32) for i in range(2)]
    epsb = sbt(root, "epsb", [128, 1], F32)
    guW = [[sbt(root, f"guW{a}{i}", [128, 16 * 128], BF16) for i in range(3)]
           for a in range(2)]
    qkv_sc = ExitStack()
    masks = sbt(qkv_sc, "masks", [128, 4 * 512], BF16)
    QT = sbt(qkv_sc, "QT", [128, 2 * S], BF16)
    KT = sbt(qkv_sc, "KT", [128, 2 * S], BF16)
    Vb = sbt(qkv_sc, "Vb", [128, NT * 256], BF16)

    semh = {}
    for s_ in ["tp", "ap", "vp", "pp", "cc", "sto", "vf", "af", "pf"]:
        semh[s_] = root.enter_context(nc.semaphore(s_))
    NROT = 8
    for s_ in ["qx", "qw"]:
        for i_ in range(NROT):
            semh[f"{s_}{i_}"] = root.enter_context(nc.semaphore(f"{s_}{i_}"))

    _fcnt = {"vf": 0, "af": 0, "pf": 0}

    def FV(v, inst):
        _fcnt["vf"] += 1
        inst.then_inc(semh["vf"], 1)
        v.wait_ge(semh["vf"], _fcnt["vf"])

    def FA(a, inst):
        _fcnt["af"] += 1
        inst.then_inc(semh["af"], 1)
        a.wait_ge(semh["af"], _fcnt["af"])

    def FP(p, inst):
        _fcnt["pf"] += 1
        inst.then_inc(semh["pf"], 1)
        p.wait_ge(semh["pf"], _fcnt["pf"])

    ev = EV()
    A = ev.add

    def dma(e, out, in_):
        return e.dma_start(out=out, in_=in_)

    A("G", lambda g: dma(g, identb[:, :], ident_in[:, :]), inc=("qw", 16, ("w_id",)))
    A("G", lambda g: dma(g, identf[:, :], ident_in[:, :]), inc=("qw", 16, ("w_idf",)))
    A("G", lambda g: dma(g, masks[:, :], mask_in[:, :]), inc=("qw", 16, ("w_const",)))
    A("V", lambda v: v.memset(onesb[:, :], 1.0), inc=("vp", 1, None))
    A("V", lambda v: v.memset(ones1b[:, :], 1.0), inc=("vp", 1, None))
    A("V", lambda v: v.memset(onesf1[:, :], 1.0), inc=("vp", 1, None))
    A("V", lambda v: v.memset(mq2[:, :], 0.0), inc=("vp", 1, None))
    A("V", lambda v: v.memset(mqr[:, :], 0.0), inc=("vp", 1, None))
    A("V", lambda v: v.memset(epsb[:, :], EPS), inc=("vp", 1, ("init",)))

    p1 = ExitStack()
    wq_sb = sbt(p1, "wq_sb", [128, 16 * 256], BF16)
    wk_sb = sbt(p1, "wk_sb", [128, 16 * 256], BF16)
    wv_sb = sbt(p1, "wv_sb", [128, 16 * 256], BF16)
    xtb = [sbt(p1, f"xtb{i}", [128, 16 * 128], BF16) for i in range(2)]
    c2s = [sbt(p1, f"c2s{i}", [128, 256], BF16) for i in range(2)]
    s2s = [sbt(p1, f"s2s{i}", [128, 256], BF16) for i in range(2)]
    rt0 = sbt(p1, "rt0", [128, 256], F32)
    rt1 = sbt(p1, "rt1", [128, 256], F32)
    rt2 = sbt(p1, "rt2", [128, 256], F32)
    qsf = sbt(p1, "qsf", [128, 256], F32)
    ksf = sbt(p1, "ksf", [128, 256], F32)
    vsf = sbt(p1, "vsf", [128, 256], F32)
    nrmb = [sbt(p1, f"nrmb{i}", [128, 1], F32) for i in range(2)]
    cvb = [sbt(p1, f"cvb{i}", [128, 256], BF16) for i in range(3)]
    colr = sbt(p1, "colr", [1, 256], BF16)
    qr = [sbt(p1, f"qr{i}", [128, 256], BF16) for i in range(2)]
    kr = [sbt(p1, f"kr{i}", [128, 256], BF16) for i in range(2)]
    Qp = [pst(p1, f"Qp{i}", [128, 256]) for i in range(2)]
    Kp = [pst(p1, f"Kp{i}", [128, 256]) for i in range(2)]
    Vp = [pst(p1, f"Vp{i}", [128, 256]) for i in range(2)]
    ptB = [pst(p1, f"ptB{i}", [128, 512]) for i in range(2)]

    for nm, src, dst in [("wq", wq_s, wq_sb), ("wk", wk_s, wk_sb), ("wv", wv_s, wv_sb)]:
        A("G", (lambda s_, d_: lambda g: dma(
            g, d_[:, :].rearrange("p (c d) -> p c d", c=16),
            s_.ap().rearrange("(c p) d -> p c d", p=128)))(src, dst),
          inc=("qw", 16, ("w_" + nm,)))

    for wi, wsb_ in enumerate([wq_sb, wk_sb, wv_sb]):
        def f_cs1(wsb_=wsb_):
            def f(te):
                for k in range(16):
                    m = te.matmul(ptB[0][0:1, 0:256], onesb[:, :],
                                  wsb_[:, 256 * k:256 * (k + 1)],
                                  start=(k == 0), stop=(k == 15))
                return m
            return f
        w = [("qw", ("w_" + ["wq", "wk", "wv"][wi],)), ("vp", ("init",))]
        if wi > 0:
            w.append(("ap", ("cve", wi - 1)))
        A("T", f_cs1(), waits=w, inc=("tp", 1, ("cs1", wi)))

        def f_cs2(wi=wi):
            return lambda a: a.copy(colr[:, :], ptB[0][0:1, 0:256])
        A("A", f_cs2(), waits=[("tp", ("cs1", wi))], inc=("ap", 1, ("cs2", wi)))

        def f_cs3(wi=wi):
            return lambda te: te.matmul(ptB[0][:, 256:512], ones1b[:, :], colr[:, :],
                                        start=True, stop=True)
        A("T", f_cs3(), waits=[("ap", ("cs2", wi))], inc=("tp", 1, ("cs3", wi)))

        def f_cve(wi=wi):
            return lambda a: a.copy(cvb[wi][:, :], ptB[0][:, 256:512])
        A("A", f_cve(), waits=[("tp", ("cs3", wi))],
          inc=("ap", 1, ("cve", wi)))
    A("T", lambda te: te.nop(), waits=[("ap", ("cve", 2))], inc=("tp", 1, ("cvb",)))

    for t in range(NT + 1):
        tb = t % 2
        u = t - 1
        ub = u % 2

        if t < NT:
            def f_dx(t=t, tb=tb):
                return lambda e: dma(e, xf[tb][:, :], x_full[t * 128:(t + 1) * 128, :])
            w = []
            if t >= 2:
                w = [("vp", ("v1", t - 2)), ("ap", ("sq", t - 2))]
            A("S", f_dx(), waits=w, inc=("qx", 16, ("x", t)))

            def f_dxt(t=t, tb=tb):
                def f(e):
                    return dma(e, xtb[tb][:, :].rearrange("p (c tk) -> p c tk", c=16),
                               x_t[:, t * 128:(t + 1) * 128]
                               .rearrange("(c p) tk -> p c tk", p=128))
                return f
            w = [("tp", ("qkv", t - 2))] if t >= 2 else []
            A("G", f_dxt(), waits=w, inc=("qw", 16, ("xt", t)))

            def f_dc(t=t, tb=tb):
                return lambda e: dma(e, c2s[tb][:, :], c2_t[t * 128:(t + 1) * 128, :])
            def f_ds(t=t, tb=tb):
                return lambda e: dma(e, s2s[tb][:, :], s2_t[t * 128:(t + 1) * 128, :])
            w = [("vp", ("ropek", t - 2))] if t >= 2 else []
            A("G", f_dc(), waits=w, inc=("qw", 16, ("tabc", t)))
            A("G", f_ds(), waits=[], inc=("qw", 16, ("tab", t)))

            def f_sq(t=t, tb=tb):
                return lambda e: e.activation(dump[:, :], xf[tb][:, :], AF.Square,
                                              accum_out=ssq[tb][:, :])
            A("A", f_sq(), waits=[("qx", ("x", t))], inc=("ap", 1, ("sq", t)))

            def f_qkv(t=t, tb=tb):
                def f(te):
                    for (ps, wsb) in [(Qp[t % 2], wq_sb), (Kp[t % 2], wk_sb),
                                      (Vp[t % 2], wv_sb)]:
                        for k in range(16):
                            m = te.matmul(ps[:, :], xtb[tb][:, 128 * k:128 * (k + 1)],
                                          wsb[:, 256 * k:256 * (k + 1)],
                                          start=(k == 0), stop=(k == 15))
                    return m
                return f
            w = [("qw", ("xt", t)), ("qw", ("w_wq",)), ("qw", ("w_wk",)),
                 ("qw", ("w_wv",)), ("tp", ("cvb",))]
            if t >= 2:
                w.append(("ap", ("scl", t - 2)))
            A("T", f_qkv(), waits=w, inc=("tp", 1, ("qkv", t)))

        if u >= 0:
            def f_nrm(u=u, ub=ub):
                def f(v):
                    FV(v, v.tensor_scalar(nrmb[ub][:, :], meanb[ub][:, :],
                                          rstd[ub][:, :], -1.0,
                                          ALU.mult, ALU.mult))
                    return v.engine_nop()
                return f
            A("V", f_nrm(), waits=[("ap", ("rstd", u))], inc=("vp", 1, ("nrm", u)))

        if t < NT:
            def f_red(t=t, tb=tb):
                def f(v):
                    r_ = v.reduce_sum(srow[tb][:, :], xf[tb][:, :], AX)
                    if t == 0:
                        FV(v, r_)
                        r_ = v.engine_nop()
                    return r_
                return f
            A("V", f_red(), waits=[("qx", ("x", t))], inc=("vp", 1, None))

        if u >= 0:
            def f_scl(ps, dst, u=u, ub=ub):
                return lambda a: a.mul(dst[:, :], ps[:, :], rstd[ub][:, :])
            wA = [("tp", ("qkv", u)), ("ap", ("rstd", u))]
            if u >= 1:
                wA.append(("vp", ("ropek", u - 1)))
            A("A", f_scl(Qp[u % 2], qsf), waits=wA, inc=("ap", 1, None))
            A("A", f_scl(Kp[u % 2], ksf), waits=[], inc=("ap", 1, None))
            A("A", f_scl(Vp[u % 2], vsf), waits=[], inc=("ap", 1, ("scl", u)))

            def f_rope(sf, cv, dst, u=u, ub=ub):
                def f(v):
                    v.scalar_tensor_tensor(rt0[:, :], cv[:, :], nrmb[ub][:, :],
                                           sf[:, :], ALU.mult, ALU.add)
                    v.tensor_tensor(rt1[:, :], rt0[:, :], c2s[ub][:, :], ALU.mult)
                    pin = rt0[:, :].rearrange("p (h s i) -> p h s i", h=2, s=2)
                    sin_ = s2s[ub][:, :].rearrange("p (h s i) -> p h s i", h=2, s=2)
                    rto = rt2[:, :].rearrange("p (h s i) -> p h s i", h=2, s=2)
                    v.tensor_tensor(rto[:, :, 0, :], pin[:, :, 1, :],
                                    sin_[:, :, 0, :], ALU.mult)
                    FV(v, v.tensor_tensor(rto[:, :, 1, :], pin[:, :, 0, :],
                                          sin_[:, :, 1, :], ALU.mult))
                    return v.tensor_tensor(dst[:, :], rt1[:, :], rt2[:, :], ALU.add)
                return f
            wq_ = [("ap", ("scl", u)), ("qw", ("tab", u)), ("qw", ("tabc", u))]
            if u >= 2:
                wq_ += [("tp", ("trqk", u - 2)), ("ap", ("nsq", u - 2))]
            A("V", f_rope(qsf, cvb[0], qr[u % 2]), waits=wq_,
              inc=("vp", 1, ("ropeq", u)))

        if t < NT:
            def f_mean(t=t, tb=tb):
                def f(v):
                    FV(v, v.tensor_scalar(meanb[tb][:, :], srow[tb][:, :],
                                          1.0 / D, None, ALU.mult))
                    return v.engine_nop()
                return f
            A("V", f_mean(), waits=[], inc=("vp", 1, None))

        if u >= 0:
            A("V", f_rope(ksf, cvb[1], kr[u % 2]), waits=[],
              inc=("vp", 1, ("ropek", u)))

            def f_nsq(u=u):
                def f(a):
                    for src_, col in ((qr[u % 2], 0), (kr[u % 2], 1)):
                        for h in range(2):
                            m = a.activation(
                                dump[:, 128 * (2 * col + h):128 * (2 * col + h + 1)],
                                src_[:, 128 * h:128 * (h + 1)], AF.Square,
                                accum_out=nsqb[:, 2 * col + h:2 * col + h + 1])
                    return m
                return f
            A("A", f_nsq(), waits=[("vp", ("ropek", u))], inc=("ap", 1, ("nsq", u)))

        if t < NT:
            def f_m2(t=t, tb=tb):
                return lambda v: v.tensor_scalar(m2b[:, :], meanb[tb][:, :],
                                                 meanb[tb][:, :], None, ALU.mult)
            A("V", f_m2(), waits=[], inc=("vp", 1, None))

        if u >= 0:
            def f_vev(u=u, ub=ub):
                def f(v):
                    return v.scalar_tensor_tensor(Vb[:, u * 256:(u + 1) * 256],
                                                  cvb[2][:, :], nrmb[ub][:, :],
                                                  vsf[:, :], ALU.mult, ALU.add)
                return f
            A("V", f_vev(), waits=[("ap", ("scl", u))], inc=("vp", 1, ("vev", u)))

            if u + 1 < NT:
                def f_ve1(t=u + 1, tb=(u + 1) % 2):
                    return lambda v: v.tensor_scalar(vepsb[:, :], ssq[tb][:, :],
                                                     1.0 / D, None, ALU.mult)
                A("V", f_ve1(), waits=[("ap", ("sq", u + 1))], inc=("vp", 1, None))

            def f_nmax(u=u):
                def f(v):
                    v.tensor_tensor(nsqb[:, 4:5], nsqb[:, 0:1], nsqb[:, 1:2], ALU.max)
                    v.tensor_tensor(nsqb[:, 5:6], nsqb[:, 2:3], nsqb[:, 3:4], ALU.max)
                    v.tensor_tensor(mqr[:, 0:1], mqr[:, 0:1], nsqb[:, 4:5], ALU.max)
                    return v.tensor_tensor(mqr[:, 1:2], mqr[:, 1:2], nsqb[:, 5:6],
                                           ALU.max)
                return f
            A("V", f_nmax(), waits=[("ap", ("nsq", u)), ("vp", ("init",))],
              inc=("vp", 1, ("nmax", u)))

        if t == 0:
            def f_ve1z(v):
                r_ = v.tensor_scalar(vepsb[:, :], ssq[0][:, :], 1.0 / D,
                                     None, ALU.mult)
                FV(v, r_)
                return v.engine_nop()
            A("V", f_ve1z, waits=[("ap", ("sq", 0))], inc=("vp", 1, None))

        if u >= 0:
            def f_trqk(u=u):
                def f(te):
                    for h in range(2):
                        te.matmul(ptB[u % 2][:, 128 * h:128 * (h + 1)],
                                  qr[u % 2][:, 128 * h:128 * (h + 1)], identb[:, :],
                                  start=True, stop=True)
                    for h in range(2):
                        m = te.matmul(ptB[u % 2][:, 256 + 128 * h:256 + 128 * (h + 1)],
                                      kr[u % 2][:, 128 * h:128 * (h + 1)], identb[:, :],
                                      start=True, stop=True)
                    return m
                return f
            w = [("vp", ("ropeq", u)), ("vp", ("ropek", u)), ("qw", ("w_id",))]
            if u >= 2:
                w.append(("ap", ("qktev", u - 2)))
            A("T", f_trqk(), waits=w, inc=("tp", 1, ("trqk", u)))

        if t < NT:
            def f_ve2(t=t, tb=tb):
                return lambda v: v.scalar_tensor_tensor(vepsb[:, :], m2b[:, :], -1.0,
                                                        vepsb[:, :], ALU.mult, ALU.add)
            A("V", f_ve2(), waits=[], inc=("vp", 1, ("v1", t)))

            def f_rstd(t=t, tb=tb):
                def f(e):
                    FA(e, e.activation(lnvb[tb][:, :], vepsb[:, :], AF.Ln,
                                       bias=epsb[:, :]))
                    return e.activation(rstd[tb][:, :], lnvb[tb][:, :], AF.Exp,
                                        scale=-0.5)
                return f
            A("A", f_rstd(), waits=[("vp", ("v1", t))], inc=("ap", 1, ("rstd", t)))

        if u >= 0:
            def f_qktev(u=u):
                def f(a):
                    qdst = QT[:, :].rearrange("p (h tk) -> p h tk", h=2)[:, :, u * 128:(u + 1) * 128]
                    kdst = KT[:, :].rearrange("p (h tk) -> p h tk", h=2)[:, :, u * 128:(u + 1) * 128]
                    qsrc = ptB[u % 2][:, 0:256].rearrange("p (h x) -> p h x", h=2)
                    ksrc = ptB[u % 2][:, 256:512].rearrange("p (h x) -> p h x", h=2)
                    a.copy(qdst, qsrc)
                    return a.copy(kdst, ksrc)
                return f
            A("A", f_qktev(), waits=[("tp", ("trqk", u))], inc=("ap", 1, ("qktev", u)))

    A("V", lambda v: v.tensor_copy(mqrb[:, :], mqr[:, :]),
      waits=[("vp", ("nmax", NT - 1))], inc=("vp", 1, ("mqb",)))
    def f_ntr(te):
        te.matmul(ptB[0][0:1, 0:128], mqrb[:, 0:1], identb[:, :],
                  start=True, stop=True)
        return te.matmul(ptB[0][0:1, 128:256], mqrb[:, 1:2], identb[:, :],
                         start=True, stop=True)
    A("T", f_ntr, waits=[("vp", ("mqb",)), ("vp", ("qktev", NT - 2))],
      inc=("tp", 1, ("ntr",)))

    def f_nred(v):
        FV(v, v.reduce_max(mq2[0:1, 0:1], ptB[0][0:1, 0:128], AX))
        FV(v, v.reduce_max(mq2[0:1, 1:2], ptB[0][0:1, 128:256], AX))
        return v.tensor_tensor(sc1[:, 1:2], mq2[:, 0:1], mq2[:, 1:2], ALU.mult)
    A("V", f_nred, waits=[("tp", ("ntr",))], inc=("vp", 1, ("b1",)))

    def f_b2(e):
        FA(e, e.activation(sc1[:, 2:3], sc1[:, 1:2], AF.Ln))
        return e.activation(sc1[:, 4:5], sc1[:, 2:3], AF.Exp, scale=0.5)
    A("A", f_b2, waits=[("vp", ("b1",))], inc=("ap", 1, ("b2",)))

    def f_b3(v):
        FV(v, v.tensor_scalar(sc1[:, 3:4], sc1[:, 4:5], -1.002 * SCL, -1.0 * SCL,
                              ALU.mult, ALU.add))
        return v.tensor_copy(sc1b[:, :], sc1[:, 3:4])
    A("V", f_b3, waits=[("ap", ("b2",))], inc=("vp", 1, ("b3",)))

    A("T", lambda te: te.matmul(ptB[0][:, 0:1], ones1b[:, :], sc1b[:, :],
                                start=True, stop=True),
      waits=[("vp", ("b3",))], inc=("tp", 1, ("b4",)))

    A("V", lambda v: v.tensor_copy(bias_sb[:, :], ptB[0][:, 0:1]),
      waits=[("tp", ("b4",))], inc=("vp", 1, ("bias",)))

    p1.close()

    p2a = ExitStack()
    attnT = sbt(p2a, "attnT", [128, 2 * S], BF16)
    PTs = sbt(p2a, "PTs", [128, 2 * 2048], BF16)
    otsb = sbt(p2a, "otsb", [128, 512], BF16)
    arow = sbt(p2a, "arow", [128, 512], BF16)
    rrow = sbt(p2a, "rrow", [1, 512], F32)
    rrow2 = sbt(p2a, "rrow2", [1, 512], F32)
    rtl = sbt(p2a, "rtl", [128, 4], F32)
    stp = pst(p2a, "stp", [128, 2048])
    outp = [pst(p2a, f"outp{i}", [128, 512]) for i in range(2)]
    sump = pst(p2a, "sump", [1, 512])
    ptr1 = pst(p2a, "ptr1", [128, 512])

    quads = []
    groups = []
    for g in range(NB):
        for h in range(2):
            groups.append((g, h))
            for qd in range(g + 1):
                quads.append((g, h, qd))
    qdidx = {x: i for i, x in enumerate(quads)}
    gidx = {gh: i for i, gh in enumerate(groups)}

    pending_oav = []
    pending_dance = [None]

    def emit_dance(gh):
        g, h = gh
        gi = gidx[gh]

        def f_rtr(g=g, h=h, gi=gi):
            def f(te):
                for j in range(4):
                    te.matmul(ptr1[:, 128 * j:128 * (j + 1)],
                              otsb[:, 128 * j:128 * (j + 1)], identb[:, :],
                              start=True, stop=True)
                for j in range(4):
                    m = te.matmul(outp[gi % 2][:, 504 + j:505 + j],
                                  rrow2[:, 128 * j:128 * (j + 1)], onesf1[:, :],
                                  start=True, stop=True)
                return m
            return f
        A("T", f_rtr(), waits=[("vp", ("sev", g, h))], inc=("tp", 1, ("rtr", g, h)))

        def f_rsc(g=g, h=h, gi=gi):
            def f(v):
                FV(v, v.tensor_copy(rtl[:, :], outp[gi % 2][:, 504:508]))
                for j in range(4):
                    m = v.tensor_scalar(arow[:, 128 * j:128 * (j + 1)],
                                        ptr1[:, 128 * j:128 * (j + 1)],
                                        rtl[:, j:j + 1], None, ALU.mult)
                return m
            return f
        A("V", f_rsc(), waits=[("tp", ("rtr", g, h))], inc=("vp", 1, ("rsc", g, h)))

        def f_tb(g=g, h=h):
            def f(te):
                for j in range(4):
                    m = te.matmul(ptr1[:, 128 * j:128 * (j + 1)],
                                  arow[:, 128 * j:128 * (j + 1)], identb[:, :],
                                  start=True, stop=True)
                return m
            return f
        A("T", f_tb(), waits=[("vp", ("rsc", g, h))], inc=("tp", 1, ("tb", g, h)))

        def f_fin(g=g, h=h):
            return lambda v: v.tensor_copy(
                attnT[:, h * S + g * 512: h * S + (g + 1) * 512], ptr1[:, :])
        A("V", f_fin(), waits=[("tp", ("tb", g, h))], inc=("vp", 1, ("fin", g, h)))

    def flush_oav():
        fo, wo_, io_ = pending_oav.pop(0)
        A("T", fo, waits=wo_, inc=io_)
        if pending_dance[0] is not None:
            emit_dance(pending_dance[0])
            pending_dance[0] = None

    for g in range(NB):
        nq = g + 1
        for h in range(2):
            gi = gidx[(g, h)]
            for qd in range(nq):
                Q = qdidx[(g, h, qd)]
                masked = qd == nq - 1

                def f_st(g=g, h=h, qd=qd, Q=Q):
                    def f(te):
                        for s_ in range(4):
                            c = 4 * qd + s_
                            m = te.matmul(
                                stp[:, 512 * s_:512 * (s_ + 1)],
                                KT[:, h * S + c * 128: h * S + (c + 1) * 128],
                                QT[:, h * S + g * 512: h * S + (g + 1) * 512],
                                start=True, stop=True)
                        return m
                    return f
                w = []
                if Q >= 1:
                    w.append(("ap", ("exp", Q - 1)))
                if Q == 0:
                    w.append(("vp", ("qktev", NT - 1)))
                A("T", f_st(), waits=w, inc=("tp", 1, ("st", Q)))

                def f_exp(Q=Q):
                    return lambda e: e.activation(
                        PTs[:, (Q % 2) * 2048:(Q % 2 + 1) * 2048],
                        stp[:, :], AF.Exp, bias=bias_sb[:, :], scale=SCL)
                w = [("tp", ("st", Q)), ("vp", ("bias",))]
                if Q >= 2:
                    w.append(("tp", ("av", Q - 2)))
                A("A", f_exp(), waits=w, inc=("ap", 1, ("exp", Q)))

                if masked:
                    def f_msk(Q=Q):
                        def f(v):
                            sl = PTs[:, (Q % 2) * 2048:(Q % 2 + 1) * 2048]
                            return v.tensor_tensor(sl, sl, masks[:, :], ALU.mult)
                        return f
                    A("V", f_msk(), waits=[("ap", ("exp", Q)), ("qw", ("w_const",))],
                      inc=("vp", 1, ("msk", Q)))

                def f_oav(g=g, h=h, qd=qd, Q=Q, gi=gi, nq=nq):
                    def f(te):
                        for s_ in range(4):
                            c = 4 * qd + s_
                            pt = PTs[:, (Q % 2) * 2048 + 512 * s_:
                                     (Q % 2) * 2048 + 512 * (s_ + 1)]
                            te.matmul(sump[:, :], onesb[:, :], pt,
                                      start=(qd == 0 and s_ == 0),
                                      stop=(qd == nq - 1 and s_ == 3),
                                      skip_group_check=True)
                            m = te.matmul(
                                outp[gi % 2][:, :],
                                Vb[:, c * 256 + h * 128: c * 256 + (h + 1) * 128],
                                pt,
                                start=(qd == 0 and s_ == 0),
                                stop=(qd == nq - 1 and s_ == 3),
                                skip_group_check=True)
                        return m
                    return f
                w = [("vp", ("msk", Q))] if masked else [("ap", ("exp", Q))]
                if qd == 0:
                    if gi >= 1:
                        w.append(("vp", ("sev1", groups[gi - 1])))
                    if gi >= 2:
                        w.append(("vp", ("rsc", *groups[gi - 2])))
                pending_oav.append((f_oav(), w, ("tp", 1, ("av", Q))))
                if len(pending_oav) > 1:
                    flush_oav()

            while pending_oav:
                flush_oav()

            ql = qdidx[(g, h, nq - 1)]

            def f_seva(g=g, h=h):
                def f(v):
                    FV(v, v.tensor_copy(rrow[:, :], sump[:, :]))
                    return v.reciprocal(rrow2[:, :], rrow[:, :])
                return f
            A("V", f_seva(), waits=[("tp", ("av", ql))],
              inc=("vp", 1, ("sev1", (g, h))))

            def f_sevb(g=g, h=h, gi=gi):
                return lambda v: v.tensor_copy(otsb[:, :], outp[gi % 2][:, :])
            A("V", f_sevb(), waits=[], inc=("vp", 1, ("sev", g, h)))

            pending_dance[0] = (g, h)

    emit_dance(pending_dance[0])
    pending_dance[0] = None

    if _DEBUG:
        A("S", lambda e: dma(e, dbg_qt[:, :], QT[:, :]),
          waits=[("vp", ("qktev", NT - 1))], inc=("sto", 16, None))
        A("S", lambda e: dma(e, dbg_kt[:, :], KT[:, :]),
          waits=[], inc=("sto", 16, None))
        A("S", lambda e: dma(e, dbg_vb[:, :], Vb[:, :]),
          waits=[], inc=("sto", 16, None))
        A("S", lambda e: e.dma_start(out=dbg_bias[:, 0:1], in_=bias_sb[:, :],
                                     allow_slow_non_contiguous=True),
          waits=[("vp", ("bias",))], inc=("sto", 16, None))
        A("S", lambda e: dma(e, dbg_bias[0:1, 1:3], mq2[:, :]),
          waits=[], inc=("sto", 16, None))
        A("S", lambda e: dma(e, dbg_at[:, :], attnT[:, :]),
          waits=[("vp", ("fin", NB - 1, 1))], inc=("sto", 16, None))
    for j in range(NCORE):
        def f_stg(j=j):
            def f(e):
                src = attnT[:, :].rearrange("p (h tk) -> p h tk", h=2)[:, :, j * 1024:(j + 1) * 1024]
                dst = a2a_in[j * 256:(j + 1) * 256, :].rearrange("(h p) tk -> p h tk", p=128)
                return dma(e, dst, src)
            return f
        A("S", f_stg(), waits=[("vp", ("fin", NB - 1, 1))], inc=("qx", 16, ("stg", j)))

    for m_pre in range(2):
        for a_pre, srcw in ((0, eg_w), (1, eu_w)):
            def f_pre(m_pre=m_pre, a_pre=a_pre, srcw=srcw):
                def f(g):
                    return dma(g, guW[a_pre][m_pre][:, :]
                               .rearrange("p (c i) -> p c i", c=16),
                               srcw[0][:, m_pre * 128:(m_pre + 1) * 128]
                               .rearrange("(c p) i -> p c i", p=128))
                return f
            key = (("guw", ("e", 0), m_pre) if a_pre == 1
                   else ("guwg", ("e", 0), m_pre))
            A("G", f_pre(), waits=[], inc=("qw", 16, key))

    def f_cc(g):
        with allow_ici_all_to_all():
            return g.collective_compute(
                "AllToAll", ALU.bypass, replica_groups=[list(range(NCORE))],
                ins=[a2a_in.ap().opt()], outs=[a2a_out.ap().opt()])
    A("G", f_cc, waits=[("qx", ("stg", j_)) for j_ in range(NCORE)],
      inc=("cc", 1, ("cc",)))

    if _DEBUG:
        A("S", lambda e: dma(e, dbg_a2a[:, :], a2a_out.ap()),
          waits=[("cc", 1)], inc=("sto", 16, None))
    p2a.close()
    qkv_sc.close()

    p2b = ExitStack()
    acc = sbt(p2b, "acc", [128, NTO * D], F32)
    comb = sbt(p2b, "comb", [128, NTO * 8], F32)
    gwsb = sbt(p2b, "gwsb", [128, 16 * 8], F32)
    lgsb = sbt(p2b, "lgsb", [128, 8], F32)
    cm1 = sbt(p2b, "cm1", [128, 1], F32)
    cm1n = sbt(p2b, "cm1n", [128, 1], F32)
    cm2 = sbt(p2b, "cm2", [128, 1], F32)
    ceq = sbt(p2b, "ceq", [128, 8], F32)
    ce1 = sbt(p2b, "ce1", [128, 8], F32)
    cge = sbt(p2b, "cge", [128, 8], F32)
    cden = sbt(p2b, "cden", [128, 1], F32)
    cdenr = sbt(p2b, "cdenr", [128, 1], F32)
    tmp8 = sbt(p2b, "tmp8", [128, 8], F32)
    gu_sc = ExitStack()
    xnTo = sbt(gu_sc, "xnTo", [128, 16 * TOK], BF16)
    tsil = [sbt(gu_sc, f"tsil{i}", [128, 512], F32) for i in range(2)]

    ln2 = ExitStack()
    xnf = [sbt(ln2, f"xnf{i}", [128, D], F32) for i in range(2)]
    xnTf = sbt(ln2, "xnTf", [128, 16 * 128], F32)
    ptA2 = [pst(ln2, f"ptA2{i}", [128, 512]) for i in range(2)]
    lgps = pst(ln2, "lgps", [128, 8])
    ptF = pst(ln2, "ptF", [128, 512])

    A("S", lambda e: dma(e, gwsb[:, :].rearrange("p (c e) -> p c e", c=16),
                         gate_wTd.ap().rearrange("(c p) e -> p c e", p=128)),
      waits=[("qx", ("stg", j_)) for j_ in range(NCORE)], inc=("qx", 16, ("gw",)))
    A("S", lambda e: dma(e, acc[:, :].rearrange("p (t f) -> p t f", t=NTO),
                         x_own.ap().rearrange("(t p) f -> p t f", p=128)),
      waits=[("qx", ("stg", j_)) for j_ in range(NCORE)],
      inc=("qx", 16, ("acc0",)))

    for t in range(NTO):
        tb = t % 2
        gt = 1024 + t

        def f_dx2(t=t, tb=tb):
            return lambda e: dma(e, xf[tb][:, :], x_own[t * 128:(t + 1) * 128, :])
        if t >= 2:
            w = [("vp", ("apply", gt - 2)), ("ap", ("sq", gt - 2))]
        else:
            w = [("vp", ("v1", 62 + t)), ("ap", ("sq", 62 + t))]
        A("G", f_dx2(), waits=w, inc=("qw", 16, ("x", gt)))

        def f_sq2(t=t, tb=tb):
            return lambda e: e.activation(dump[:, :], xf[tb][:, :], AF.Square,
                                          accum_out=ssq[tb][:, :])
        A("A", f_sq2(), waits=[("qx", ("x", gt))], inc=("ap", 1, ("sq", gt)))

        def f_stats2(t=t, tb=tb):
            def f(v):
                FV(v, v.reduce_sum(srow[tb][:, :], xf[tb][:, :], AX))
                FV(v, v.tensor_scalar(meanb[tb][:, :], srow[tb][:, :], 1.0 / D,
                                      None, ALU.mult))
                FV(v, v.tensor_scalar(m2b[:, :], meanb[tb][:, :], meanb[tb][:, :],
                                      None, ALU.mult))
                FV(v, v.tensor_scalar(vepsb[:, :], ssq[tb][:, :], 1.0 / D, None,
                                      ALU.mult))
                FV(v, v.tensor_tensor(vepsb[:, :], vepsb[:, :], m2b[:, :],
                                      ALU.subtract))
                return v.tensor_scalar(vepsb[:, :], vepsb[:, :], EPS, None, ALU.add)
            return f
        A("V", f_stats2(), waits=[("qx", ("x", gt)), ("ap", ("sq", gt))],
          inc=("vp", 1, ("v1", gt)))

        def f_rstd2(t=t, tb=tb):
            def f(e):
                FA(e, e.activation(lnvb[tb][:, :], vepsb[:, :], AF.Ln))
                return e.activation(rstd[tb][:, :], lnvb[tb][:, :], AF.Exp, scale=-0.5)
            return f
        A("A", f_rstd2(), waits=[("vp", ("v1", gt))], inc=("ap", 1, ("rstd", gt)))

        def f_apply2(t=t, tb=tb):
            def f(v):
                v.tensor_scalar(xn[tb][:, :], xf[tb][:, :], meanb[tb][:, :],
                                rstd[tb][:, :], ALU.subtract, ALU.mult)
                return v.tensor_scalar(xnf[tb][:, :], xf[tb][:, :], meanb[tb][:, :],
                                       rstd[tb][:, :], ALU.subtract, ALU.mult)
            return f
        A("V", f_apply2(), waits=[("ap", ("rstd", gt))], inc=("vp", 1, ("apply", gt)))

        for j in range(4):
            gi = 1024 + t * 4 + j

            def f_tr2(t=t, tb=tb, j=j, gi=gi):
                def f(te):
                    for i in range(4):
                        c = 4 * j + i
                        m = te.matmul(ptA2[gi % 2][:, 128 * i:128 * (i + 1)],
                                      xn[tb][:, 128 * c:128 * (c + 1)],
                                      identb[:, :], start=True, stop=True)
                    return m
                return f
            w = [("vp", ("apply", gt))]
            if gi >= 1026:
                w.append(("vp", ("trev", gi - 2)))
            A("T", f_tr2(), waits=w, inc=("tp", 1, ("trb", gi)))

            def f_trev2(t=t, j=j, gi=gi):
                def f(v):
                    src = ptA2[gi % 2][:, :].rearrange("p (c i) -> p c i", c=4)
                    dst = xnTo[:, :].rearrange("p (c tk) -> p c tk", c=16)[
                        :, 4 * j:4 * j + 4, t * 128:(t + 1) * 128]
                    return v.tensor_copy(dst, src)
                return f
            A("V", f_trev2(), waits=[("tp", ("trb", gi))], inc=("vp", 1, ("trev", gi)))

        for j in range(4):
            def f_trfj(t=t, tb=tb, j=j):
                def f(te):
                    for i in range(4):
                        c = 4 * j + i
                        m = te.matmul(ptF[:, 128 * i:128 * (i + 1)],
                                      xnf[tb][:, 128 * c:128 * (c + 1)],
                                      identf[:, :], start=True, stop=True)
                    return m
                return f
            w = [("vp", ("apply", gt)), ("qw", ("w_idf",))]
            if j >= 1:
                w.append(("vp", ("trfe", gt, j - 1)))
            elif t >= 1:
                w.append(("tp", ("lg", t - 1)))
            A("T", f_trfj(), waits=w, inc=("tp", 1, ("trf", gt, j)))

            def f_trfe(t=t, j=j):
                def f(v):
                    return v.tensor_copy(xnTf[:, j * 512:(j + 1) * 512], ptF[:, :])
                return f
            w = [("tp", ("trf", gt, j))]
            if j == 0 and t >= 1:
                w.append(("tp", ("lg", t - 1)))
            A("V", f_trfe(), waits=w, inc=("vp", 1, ("trfe", gt, j)))

        def f_lg(t=t):
            def f(te):
                for k in range(16):
                    m = te.matmul(lgps[:, :], xnTf[:, 128 * k:128 * (k + 1)],
                                  gwsb[:, 8 * k:8 * (k + 1)],
                                  start=(k == 0), stop=(k == 15))
                return m
            return f
        w = [("vp", ("trfe", gt, 3)), ("qx", ("gw",))]
        if t >= 1:
            w.append(("vp", ("cmb1", t - 1)))
        A("T", f_lg(), waits=w, inc=("tp", 1, ("lg", t)))

        def f_comb1(t=t):
            def f(v):
                FV(v, v.tensor_copy(lgsb[:, :], lgps[:, :]))
                FV(v, v.reduce_max(cm1[:, :], lgsb[:, :], AX))
                v.tensor_scalar(cm1n[:, :], cm1[:, :], -1.0, None, ALU.mult)
                FV(v, v.tensor_scalar(ceq[:, :], lgsb[:, :], cm1[:, :], None,
                                      ALU.is_ge))
                FV(v, v.tensor_scalar(ceq[:, :], ceq[:, :], -1e30, None, ALU.mult))
                FV(v, v.tensor_tensor(ceq[:, :], ceq[:, :], lgsb[:, :], ALU.add))
                return v.reduce_max(cm2[:, :], ceq[:, :], AX)
            return f
        w = [("tp", ("lg", t))]
        if t >= 1:
            w.append(("ap", ("cmb2", t - 1)))
        A("V", f_comb1(), waits=w, inc=("vp", 1, ("cmb1", t)))

        def f_comb2(t=t):
            def f(e):
                e.activation(ce1[:, :], lgsb[:, :], AF.Exp, bias=cm1n[:, :])
                return e.activation(cden[:, :], cm2[:, :], AF.Exp, bias=cm1n[:, :])
            return f
        A("A", f_comb2(), waits=[("vp", ("cmb1", t))], inc=("ap", 1, ("cmb2", t)))

        def f_comb3(t=t):
            def f(v):
                FV(v, v.tensor_scalar(cden[:, :], cden[:, :], 1.0, None, ALU.add))
                FV(v, v.reciprocal(cdenr[:, :], cden[:, :]))
                FV(v, v.tensor_scalar(cge[:, :], lgsb[:, :], cm2[:, :], None,
                                      ALU.is_ge))
                FV(v, v.tensor_scalar(tmp8[:, :], ce1[:, :], cdenr[:, :], None,
                                      ALU.mult))
                return v.tensor_tensor(comb[:, t * 8:(t + 1) * 8], tmp8[:, :],
                                       cge[:, :], ALU.mult)
            return f
        A("V", f_comb3(), waits=[("ap", ("cmb2", t))], inc=("vp", 1, ("cmb3", t)))

    if _DEBUG:
        A("S", lambda e: dma(e, dbg_cmb[:, :], comb[:, :]),
          waits=[("vp", ("cmb3", NTO - 1))], inc=("sto", 16, None))
        A("S", lambda e: dma(e, dbg_xnto[:, :], xnTo[:, :]),
          waits=[("vp", ("trev", 1024 + (NTO - 1) * 4 + 3))], inc=("sto", 16, None))
    ln2.close()
    GU = [pst(p2b, f"GU{i}", [128, 512]) for i in range(4)]
    EPm = [pst(p2b, f"EPm{i}", [128, 512]) for i in range(4)]

    exp_sc = ExitStack()
    actT = sbt(exp_sc, "actT", [128, 11 * TOK], BF16)
    dwn = [sbt(exp_sc, f"dwn{i}", [128, 11 * 512], BF16, side="right") for i in range(2)]

    gu_hist = []
    sil_hist = []
    ep_hist = []

    def emit_gu(tag, nm, src_g, src_u, actdst, e_idx, first_extra_wait,
                preloaded=0):
        for m in range(nm):
            li = len(gu_hist)
            gu_hist.append((tag, m))
            mb = li % 3
            skip_load = m < preloaded

            def f_wg(m=m, mb=mb):
                def f(g):
                    sl = src_g.ap() if e_idx is None else src_g[e_idx]
                    return dma(g, guW[0][mb][:, :].rearrange("p (c i) -> p c i", c=16),
                               sl[:, m * 128:(m + 1) * 128]
                               .rearrange("(c p) i -> p c i", p=128))
                return f

            def f_wu(m=m, mb=mb):
                def f(g):
                    sl = src_u.ap() if e_idx is None else src_u[e_idx]
                    return dma(g, guW[1][mb][:, :].rearrange("p (c i) -> p c i", c=16),
                               sl[:, m * 128:(m + 1) * 128]
                               .rearrange("(c p) i -> p c i", p=128))
                return f
            if not skip_load:
                w = []
                if li >= 3:
                    pt_, pm_ = gu_hist[li - 3]
                    w.append(("vp", ("gact", pt_, pm_, 1)))
                A("G", f_wg(), waits=w, inc=("qw", 16, ("guwg", tag, m)))
                A("G", f_wu(), waits=[], inc=("qw", 16, ("guw", tag, m)))

            for hh in range(2):
                def f_gu(m=m, mb=mb, hh=hh):
                    def f(te):
                        for k in range(16):
                            te.matmul(GU[hh][:, :],
                                      guW[0][mb][:, 128 * k:128 * (k + 1)],
                                      xnTo[:, k * TOK + hh * 512: k * TOK + hh * 512 + 512],
                                      start=(k == 0), stop=(k == 15))
                        for k in range(16):
                            m_ = te.matmul(GU[2 + hh][:, :],
                                           guW[1][mb][:, 128 * k:128 * (k + 1)],
                                           xnTo[:, k * TOK + hh * 512: k * TOK + hh * 512 + 512],
                                           start=(k == 0), stop=(k == 15))
                        return m_
                    return f
                w = [("qw", ("guw", tag, m)), ("qw", ("guwg", tag, m))]
                if m == 0 and hh == 0 and first_extra_wait:
                    w += first_extra_wait
                if m >= 1:
                    w.append(("vp", ("gact", tag, m - 1, hh)))
                A("T", f_gu(), waits=w, inc=("tp", 1, ("gu", tag, m, hh)))

                sli = len(sil_hist)
                sil_hist.append((tag, m, hh))

                def f_sil(m=m, hh=hh, sli=sli):
                    return lambda e: e.activation(tsil[sli % 2][:, :], GU[hh][:, :],
                                                  AF.Silu)
                w = [("tp", ("gu", tag, m, hh))]
                if sli >= 2:
                    w.append(("vp", ("gact", *sil_hist[sli - 2])))
                A("A", f_sil(), waits=w, inc=("ap", 1, ("sil", tag, m, hh)))

                def f_gact(m=m, hh=hh, sli=sli, actdst=actdst):
                    def f(v):
                        return v.tensor_tensor(
                            actdst[:, m * TOK + hh * 512: m * TOK + hh * 512 + 512],
                            tsil[sli % 2][:, :], GU[2 + hh][:, :], ALU.mult)
                    return f
                A("V", f_gact(), waits=[("ap", ("sil", tag, m, hh))],
                  inc=("vp", 1, ("gact", tag, m, hh)))

    def ep_prev_mark(kind):
        if not ep_hist:
            return None
        u = ep_hist[-1]
        if u[0] == "dn":
            return ("dev", u[1], u[2], u[3], 3)
        return ("swev", u[1], u[2], 3)

    def prefetch_guw(tag, src_g, src_u, e_idx):
        for m in range(2):
            li_pred = len(gu_hist) + m
            mb = li_pred % 3
            for a_, srcw in ((0, src_g), (1, src_u)):
                def f_pre(m=m, mb=mb, a_=a_, srcw=srcw, e_idx=e_idx):
                    def f(g):
                        sl = srcw.ap() if e_idx is None else srcw[e_idx]
                        return dma(g, guW[a_][mb][:, :]
                                   .rearrange("p (c i) -> p c i", c=16),
                                   sl[:, m * 128:(m + 1) * 128]
                                   .rearrange("(c p) i -> p c i", p=128))
                    return f
                w = [("vp", ("gact", *gu_hist[li_pred - 3], 1))]
                key = ("guw", tag, m) if a_ == 1 else ("guwg", tag, m)
                A("G", f_pre(), waits=w, inc=("qw", 16, key))

    dwn_hist = []
    for e in range(8):
        emit_gu(("e", e), 11, eg_w, eu_w, actT, e,
                [("vp", ("cmb3", NTO - 1)), ("vp", ("trev", 1024 + (NTO - 1) * 4 + 3))]
                if e == 0 else None, preloaded=2)
        if _DEBUG and e == 0:
            A("S", lambda e_: dma(e_, dbg_act[:, :], actT[:, :]),
              waits=[("vp", ("gact", ("e", 0), 10, 1))], inc=("sto", 16, None))
        for n in range(4):
            dli = len(dwn_hist)
            dwn_hist.append((e, n))
            db = dli % 2

            def f_dw(e=e, n=n, db=db):
                def f(g):
                    return dma(g, dwn[db][:, :].rearrange("p (c f) -> p c f", c=11),
                               ed_w[e][:, n * 512:(n + 1) * 512]
                               .rearrange("(c p) f -> p c f", p=128))
                return f
            w = []
            if dli == 0:
                w.append(("tp", ("lg", NTO - 1)))
            if dli >= 2:
                pe_, pn_ = dwn_hist[dli - 2]
                w.append(("tp", ("dn", pe_, pn_, 1)))
            A("G", f_dw(), waits=w, inc=("qw", 16, ("dw", e, n)))

            for half in range(2):
                pm = ep_prev_mark("dn")
                ep_hist.append(("dn", e, n, half))

                def f_down(e=e, n=n, db=db, half=half):
                    def f(te):
                        for c in range(11):
                            for ti in range(4):
                                tile = half * 4 + ti
                                m = te.matmul(
                                    EPm[ti][:, :],
                                    actT[:, c * TOK + tile * 128: c * TOK + (tile + 1) * 128],
                                    dwn[db][:, c * 512:(c + 1) * 512],
                                    start=(c == 0), stop=(c == 10),
                                    skip_group_check=True)
                        return m
                    return f
                w = [("qw", ("dw", e, n)), ("vp", ("gact", ("e", e), 10, 1))]
                if pm is not None:
                    w.append(("vp", pm))
                A("T", f_down(), waits=w, inc=("tp", 1, ("dn", e, n, half)))

                for ti in range(4):
                    def f_dev(e=e, n=n, half=half, ti=ti):
                        def f(v):
                            tile = half * 4 + ti
                            sl = acc[:, tile * D + n * 512: tile * D + n * 512 + 512]
                            return v.scalar_tensor_tensor(
                                sl, EPm[ti][:, :],
                                comb[:, tile * 8 + e: tile * 8 + e + 1],
                                sl, ALU.mult, ALU.add)
                        return f
                    w = [("tp", ("dn", e, n, half)), ("qx", ("acc0",))]
                    A("V", f_dev(), waits=w, inc=("vp", 1, ("dev", e, n, half, ti)))

            if n == 1:
                if e < 7:
                    prefetch_guw(("e", e + 1), eg_w, eu_w, e + 1)
                else:
                    prefetch_guw(("s", 0), sg_w, su_w, None)

    exp_sc.close()
    sh_sc = ExitStack()
    actsh = sbt(sh_sc, "actsh", [128, 22 * TOK], BF16, side="right")

    emit_gu(("s", 0), 22, sg_w, su_w, actsh, None, None, preloaded=2)

    gu_sc.close()
    sw_sc = ExitStack()
    sdw = sbt(sw_sc, "sdw", [128, 22 * 512], BF16)
    wow = sbt(sw_sc, "wow", [128, 16 * 512], BF16)
    maw = [sbt(sw_sc, f"maw{i}", [128, TOK], BF16) for i in range(3)]

    maw_hist = []
    for n in range(4):
        def f_sdw(n=n):
            def f(g):
                return dma(g, sdw[:, :].rearrange("p (c f) -> p c f", c=22),
                           sd_w[:, n * 512:(n + 1) * 512]
                           .rearrange("(c p) f -> p c f", p=128))
            return f
        w = ([("tp", ("wo", n - 1, 1, 15))] if n >= 1
             else [("tp", ("dn", 7, 3, 1)), ("tp", ("gu", ("s", 0), 21, 1))])
        A("G", f_sdw(), waits=w, inc=("qw", 16, ("sdw", n)))

        def f_wow(n=n):
            def f(g):
                return dma(g, wow[:, :].rearrange("p (c f) -> p c f", c=16),
                           wo_full[:, n * 512:(n + 1) * 512]
                           .rearrange("(c p) f -> p c f", p=128))
            return f
        A("G", f_wow(), waits=[], inc=("qw", 16, ("wow", n)))

        for half in range(2):
            pm = ep_prev_mark("sw")
            ep_hist.append(("sw", n, half))

            def f_sd(n=n, half=half):
                def f(te):
                    for c in range(22):
                        for ti in range(4):
                            tile = half * 4 + ti
                            m = te.matmul(
                                EPm[ti][:, :],
                                actsh[:, c * TOK + tile * 128: c * TOK + (tile + 1) * 128],
                                sdw[:, c * 512:(c + 1) * 512],
                                start=(c == 0), stop=False,
                                skip_group_check=True)
                    return m
                return f
            w = [("qw", ("sdw", n)), ("vp", ("gact", ("s", 0), 21, 1))]
            if pm is not None:
                w.append(("vp", pm))
            A("T", f_sd(), waits=w, inc=("tp", 1, ("sd", n, half)))

            for jj in range(16):
                mli = len(maw_hist)
                maw_hist.append((n, half, jj))

                def f_maw(n=n, half=half, jj=jj, mli=mli):
                    return lambda e: dma(e, maw[mli % 3][:, :],
                                         a2a_out[jj * 128:(jj + 1) * 128, :])
                w = [("cc", 1)]
                if mli < 3:
                    w += [("tp", ("dn", 7, 3, 1)), ("tp", ("gu", ("s", 0), 21, 1))]
                if mli >= 3:
                    w.append(("tp", ("wo", *maw_hist[mli - 3])))
                A("S", f_maw(), waits=w, inc=("qx", 16, ("maw", n, half, jj)))

                def f_wo(n=n, half=half, jj=jj, mli=mli):
                    def f(te):
                        for ti in range(4):
                            tile = half * 4 + ti
                            m = te.matmul(
                                EPm[ti][:, :],
                                maw[mli % 3][:, tile * 128:(tile + 1) * 128],
                                wow[:, jj * 512:(jj + 1) * 512],
                                start=False, stop=(jj == 15),
                                skip_group_check=True)
                        return m
                    return f
                w = [("qx", ("maw", n, half, jj)), ("qw", ("wow", n)),
                     ("tp", ("sd", n, half))]
                A("T", f_wo(), waits=w, inc=("tp", 1, ("wo", n, half, jj)))

            for ti in range(4):
                def f_swev(n=n, half=half, ti=ti):
                    def f(v):
                        tile = half * 4 + ti
                        sl = acc[:, tile * D + n * 512: tile * D + n * 512 + 512]
                        return v.tensor_tensor(sl, sl, EPm[ti][:, :], ALU.add)
                    return f
                A("V", f_swev(), waits=[("tp", ("wo", n, half, 15))],
                  inc=("vp", 1, ("swev", n, half, ti)))

    def f_out(e):
        return dma(e, out_d.ap().rearrange("(t p) f -> p t f", p=128),
                   acc[:, :].rearrange("p (t f) -> p t f", t=NTO))
    A("S", f_out, waits=[("vp", ("swev", 3, 1, 3))], inc=("sto", 16, ("out",)))

    A("S", lambda e: e.nop(), waits=[("sto", ("out",))], inc=None)

    rot = {"qx": 0, "qw": 0}
    events2 = []
    for eng, fn, waits, inc in ev.events:
        if inc is not None and inc[0] in rot:
            sem, amt, key = inc
            phys = f"{sem}{rot[sem] % NROT}"
            rot[sem] += 1
            inc = (phys, amt, key)
        events2.append((eng, fn, waits, inc))
    ev.events = events2

    cnt = defaultdict(int)
    marks = {}
    for eng, fn, waits, inc in ev.events:
        if inc is not None:
            sem, amt, key = inc
            cnt[sem] += amt
            if key is not None:
                assert key not in marks, key
                marks[key] = (sem, cnt[sem])

    def resolve(w):
        sem, key = w
        if isinstance(key, int):
            return sem, key
        msem, val = marks[key]
        return msem, val

    with nc.Block() as block:
        def emit_for(eng_name, selector):
            def body(e):
                hw = defaultdict(int)
                for eng, fn, waits, inc in ev.events:
                    if eng != eng_name:
                        continue
                    for w_ in waits:
                        sem, val = resolve(w_)
                        if val > hw[sem]:
                            e.wait_ge(semh[sem], val)
                            hw[sem] = val
                    inst = fn(e)
                    if inc is not None:
                        sem, amt, _ = inc
                        inst.then_inc(semh[sem], amt)
            selector(body)

        emit_for("S", block.sync)
        emit_for("G", block.gpsimd)
        emit_for("T", block.tensor)
        emit_for("A", block.scalar)
        emit_for("V", block.vector)

    sw_sc.close()
    sh_sc.close()
    p2b.close()
    root.close()
    return nc


def _rope_tables():
    exps = np.arange(0, 128, 2, dtype=np.float32) / np.float32(128.0)
    inv_freq = (np.float32(1.0) / (np.float32(10000.0) ** exps)).astype(np.float32)
    pos = np.arange(S, dtype=np.float32)
    ang = (pos[:, None] * inv_freq[None, :]).astype(np.float32)
    cos = np.cos(ang.astype(np.float64)).astype(np.float32)
    sin = np.sin(ang.astype(np.float64)).astype(np.float32)
    c1 = np.concatenate([cos, cos], axis=1)
    s1 = np.concatenate([-sin, sin], axis=1)
    C2 = np.ascontiguousarray(np.concatenate([c1, c1], axis=1))
    S2 = np.ascontiguousarray(np.concatenate([s1, s1], axis=1))
    return C2, S2


def _mask_tiles():
    kk = np.arange(128)[:, None]
    qq_ = np.arange(512)[None, :]
    tiles = [(128 * m + kk <= qq_).astype(np.float32) for m in range(4)]
    return np.ascontiguousarray(np.concatenate(tiles, axis=1))


def kernel(L0_attn_x, L0_attn_wq, L0_attn_wk, L0_attn_wv, L0_attn_wo,
           L0_ffn_gate_w, L0_ffn_expert_gate, L0_ffn_expert_up,
           L0_ffn_expert_down, L0_ffn_shared_gate, L0_ffn_shared_up,
           L0_ffn_shared_down):
    x = np.ascontiguousarray(np.asarray(L0_attn_x, dtype=np.float32).reshape(S, D))
    wq = np.asarray(L0_attn_wq, dtype=np.float32)
    wk = np.asarray(L0_attn_wk, dtype=np.float32)
    wv = np.asarray(L0_attn_wv, dtype=np.float32)
    wo = np.ascontiguousarray(np.asarray(L0_attn_wo, dtype=np.float32))
    gate_w = np.asarray(L0_ffn_gate_w, dtype=np.float32)
    eg = np.asarray(L0_ffn_expert_gate, dtype=np.float32)
    eu = np.asarray(L0_ffn_expert_up, dtype=np.float32)
    edn = np.asarray(L0_ffn_expert_down, dtype=np.float32)
    sg = np.asarray(L0_ffn_shared_gate, dtype=np.float32)
    su = np.asarray(L0_ffn_shared_up, dtype=np.float32)
    sdn = np.asarray(L0_ffn_shared_down, dtype=np.float32)

    C2, S2 = _rope_tables()
    shared = dict(
        x_full=x,
        x_t=np.ascontiguousarray(x.T),
        wo_full=wo,
        c2_t=C2, s2_t=S2,
        ident_in=np.eye(128, dtype=np.float32),
        mask_in=_mask_tiles(),
        gate_wTd=np.ascontiguousarray(gate_w.T),
        eg_w=np.ascontiguousarray(np.transpose(eg, (0, 2, 1))),
        eu_w=np.ascontiguousarray(np.transpose(eu, (0, 2, 1))),
        ed_w=np.ascontiguousarray(np.transpose(edn, (0, 2, 1))),
        sg_w=np.ascontiguousarray(sg.T),
        su_w=np.ascontiguousarray(su.T),
        sd_w=np.ascontiguousarray(sdn.T),
    )
    in_maps = []
    for c in range(NCORE):
        m = dict(shared)
        m["x_own"] = np.ascontiguousarray(x[c * TOK:(c + 1) * TOK, :])
        m["wq_s"] = np.ascontiguousarray(wq[:, c * 256:(c + 1) * 256])
        m["wk_s"] = np.ascontiguousarray(wk[:, c * 256:(c + 1) * 256])
        m["wv_s"] = np.ascontiguousarray(wv[:, c * 256:(c + 1) * 256])
        in_maps.append(m)

    nc = _build()
    res = run_bass_kernel_spmd(nc, in_maps, core_ids=list(range(NCORE)))
    out = np.concatenate([res.results[c]["out"] for c in range(NCORE)], axis=0)
    return out.reshape(1, S, D)


# version 16
# speedup vs baseline: 1.0041x; 1.0041x over previous
import math
from collections import defaultdict
from contextlib import ExitStack

import numpy as np

import concourse.bass as bass
import concourse.mybir as mybir
from concourse.bass_utils import run_bass_kernel_spmd
from concourse.aot_env import allow_ici_all_to_all

F32 = mybir.dt.float32
BF16 = mybir.dt.bfloat16
AF = mybir.ActivationFunctionType
ALU = mybir.AluOpType
AX = mybir.AxisListType.X

S = 8192
D = 2048
NCORE = 8
TOK = S // NCORE
NT = S // 128
NB = S // 512
NTO = TOK // 128
EI = 1408
ISH = 2816
SCL = 1.0 / math.sqrt(128.0)
EPS = 1e-5


_DEBUG = False


class EV:
    def __init__(self):
        self.events = []

    def add(self, eng, fn, waits=(), inc=None):
        self.events.append((eng, fn, tuple(waits), inc))


def _build():
    nc = bass.Bass("TRN2", target_bir_lowering=False)

    x_full = nc.dram_tensor("x_full", [S, D], F32, kind="ExternalInput")
    x_own = nc.dram_tensor("x_own", [TOK, D], F32, kind="ExternalInput")
    x_t = nc.dram_tensor("x_t", [D, S], F32, kind="ExternalInput")
    wq_s = nc.dram_tensor("wq_s", [D, 256], F32, kind="ExternalInput")
    wk_s = nc.dram_tensor("wk_s", [D, 256], F32, kind="ExternalInput")
    wv_s = nc.dram_tensor("wv_s", [D, 256], F32, kind="ExternalInput")
    wo_full = nc.dram_tensor("wo_full", [D, D], F32, kind="ExternalInput")
    c2_t = nc.dram_tensor("c2_t", [S, 256], F32, kind="ExternalInput")
    s2_t = nc.dram_tensor("s2_t", [S, 256], F32, kind="ExternalInput")
    ident_in = nc.dram_tensor("ident_in", [128, 128], F32, kind="ExternalInput")
    mask_in = nc.dram_tensor("mask_in", [128, 4 * 512], F32, kind="ExternalInput")
    gate_wTd = nc.dram_tensor("gate_wTd", [D, 8], F32, kind="ExternalInput")
    eg_w = nc.dram_tensor("eg_w", [8, D, EI], F32, kind="ExternalInput")
    eu_w = nc.dram_tensor("eu_w", [8, D, EI], F32, kind="ExternalInput")
    ed_w = nc.dram_tensor("ed_w", [8, EI, D], F32, kind="ExternalInput")
    sg_w = nc.dram_tensor("sg_w", [D, ISH], F32, kind="ExternalInput")
    su_w = nc.dram_tensor("su_w", [D, ISH], F32, kind="ExternalInput")
    sd_w = nc.dram_tensor("sd_w", [ISH, D], F32, kind="ExternalInput")
    out_d = nc.dram_tensor("out", [TOK, D], F32, kind="ExternalOutput")

    a2a_in = nc.dram_tensor("a2a_in", [D, TOK], BF16, kind="Internal")
    a2a_out = nc.dram_tensor("a2a_out", [D, TOK], BF16, kind="Internal")
    if _DEBUG:
        dbg_qt = nc.dram_tensor("dbg_qt", [128, 2 * S], BF16, kind="ExternalOutput")
        dbg_kt = nc.dram_tensor("dbg_kt", [128, 2 * S], BF16, kind="ExternalOutput")
        dbg_vb = nc.dram_tensor("dbg_vb", [128, NT * 256], BF16, kind="ExternalOutput")
        dbg_bias = nc.dram_tensor("dbg_bias", [128, 4], F32, kind="ExternalOutput")
        dbg_at = nc.dram_tensor("dbg_at", [128, 2 * S], BF16, kind="ExternalOutput")
        dbg_a2a = nc.dram_tensor("dbg_a2a", [D, TOK], BF16, kind="ExternalOutput")
        dbg_cmb = nc.dram_tensor("dbg_cmb", [128, NTO * 8], F32, kind="ExternalOutput")
        dbg_xnto = nc.dram_tensor("dbg_xnto", [128, 16 * TOK], BF16, kind="ExternalOutput")
        dbg_act = nc.dram_tensor("dbg_act", [128, 11 * TOK], BF16, kind="ExternalOutput")

    root = ExitStack()
    def sbt(st, n, s, d, side=None):
        return st.enter_context(nc.sbuf_tensor(n, s, d, side=side))
    def pst(st, n, s, d=F32):
        return st.enter_context(nc.psum_tensor(n, s, d))

    identb = sbt(root, "identb", [128, 128], BF16)
    identf = sbt(root, "identf", [128, 128], F32)
    onesb = sbt(root, "onesb", [128, 1], BF16)
    ones1b = sbt(root, "ones1b", [1, 128], BF16)
    onesf1 = sbt(root, "onesf1", [1, 1], F32)
    bias_sb = sbt(root, "bias_sb", [128, 1], F32)
    mq2 = sbt(root, "mq2", [1, 2], F32)
    nsqb = sbt(root, "nsqb", [128, 6], F32)
    mqr = sbt(root, "mqr", [128, 2], F32)
    mqrb = sbt(root, "mqrb", [128, 2], BF16)
    sc1 = sbt(root, "sc1", [1, 6], F32)
    sc1b = sbt(root, "sc1b", [1, 1], BF16)
    xf = [sbt(root, f"xf{i}", [128, D], F32) for i in range(2)]
    xn = [sbt(root, f"xn{i}", [128, D], BF16) for i in range(2)]
    dump = sbt(root, "dump", [128, D], BF16)
    srow = [sbt(root, f"srow{i}", [128, 1], F32) for i in range(2)]
    ssq = [sbt(root, f"ssq{i}", [128, 1], F32) for i in range(2)]
    meanb = [sbt(root, f"meanb{i}", [128, 1], F32) for i in range(2)]
    m2b = sbt(root, "m2b", [128, 1], F32)
    vepsb = sbt(root, "vepsb", [128, 1], F32)
    rstd = [sbt(root, f"rstd{i}", [128, 1], F32) for i in range(2)]
    lnvb = [sbt(root, f"lnvb{i}", [128, 1], F32) for i in range(2)]
    epsb = sbt(root, "epsb", [128, 1], F32)
    guW = [[sbt(root, f"guW{a}{i}", [128, 16 * 128], BF16) for i in range(3)]
           for a in range(2)]
    qkv_sc = ExitStack()
    masks = sbt(qkv_sc, "masks", [128, 4 * 512], BF16)
    QT = sbt(qkv_sc, "QT", [128, 2 * S], BF16)
    KT = sbt(qkv_sc, "KT", [128, 2 * S], BF16)
    Vb = sbt(qkv_sc, "Vb", [128, NT * 256], BF16)

    semh = {}
    for s_ in ["tp", "ap", "vp", "pp", "cc", "sto", "vf", "af", "pf"]:
        semh[s_] = root.enter_context(nc.semaphore(s_))
    NROT = 8
    for s_ in ["qx", "qw"]:
        for i_ in range(NROT):
            semh[f"{s_}{i_}"] = root.enter_context(nc.semaphore(f"{s_}{i_}"))

    _fcnt = {"vf": 0, "af": 0, "pf": 0}

    def FV(v, inst):
        _fcnt["vf"] += 1
        inst.then_inc(semh["vf"], 1)
        v.wait_ge(semh["vf"], _fcnt["vf"])

    def FA(a, inst):
        _fcnt["af"] += 1
        inst.then_inc(semh["af"], 1)
        a.wait_ge(semh["af"], _fcnt["af"])

    def FP(p, inst):
        _fcnt["pf"] += 1
        inst.then_inc(semh["pf"], 1)
        p.wait_ge(semh["pf"], _fcnt["pf"])

    ev = EV()
    A = ev.add

    def dma(e, out, in_):
        return e.dma_start(out=out, in_=in_)

    A("G", lambda g: dma(g, identb[:, :], ident_in[:, :]), inc=("qw", 16, ("w_id",)))
    A("G", lambda g: dma(g, identf[:, :], ident_in[:, :]), inc=("qw", 16, ("w_idf",)))
    A("G", lambda g: dma(g, masks[:, :], mask_in[:, :]), inc=("qw", 16, ("w_const",)))
    A("V", lambda v: v.memset(onesb[:, :], 1.0), inc=("vp", 1, None))
    A("V", lambda v: v.memset(ones1b[:, :], 1.0), inc=("vp", 1, None))
    A("V", lambda v: v.memset(onesf1[:, :], 1.0), inc=("vp", 1, None))
    A("V", lambda v: v.memset(mq2[:, :], 0.0), inc=("vp", 1, None))
    A("V", lambda v: v.memset(mqr[:, :], 0.0), inc=("vp", 1, None))
    A("V", lambda v: v.memset(epsb[:, :], EPS), inc=("vp", 1, ("init",)))

    p1 = ExitStack()
    wq_sb = sbt(p1, "wq_sb", [128, 16 * 256], BF16)
    wk_sb = sbt(p1, "wk_sb", [128, 16 * 256], BF16)
    wv_sb = sbt(p1, "wv_sb", [128, 16 * 256], BF16)
    xtb = [sbt(p1, f"xtb{i}", [128, 16 * 128], BF16) for i in range(2)]
    c2s = [sbt(p1, f"c2s{i}", [128, 256], BF16) for i in range(2)]
    s2s = [sbt(p1, f"s2s{i}", [128, 256], BF16) for i in range(2)]
    rt0 = sbt(p1, "rt0", [128, 256], F32)
    rt1 = sbt(p1, "rt1", [128, 256], F32)
    rt2 = sbt(p1, "rt2", [128, 256], F32)
    qsf = sbt(p1, "qsf", [128, 256], F32)
    ksf = sbt(p1, "ksf", [128, 256], F32)
    vsf = sbt(p1, "vsf", [128, 256], F32)
    nrmb = [sbt(p1, f"nrmb{i}", [128, 1], F32) for i in range(2)]
    cvb = [sbt(p1, f"cvb{i}", [128, 256], BF16) for i in range(3)]
    colr = sbt(p1, "colr", [1, 256], BF16)
    qr = [sbt(p1, f"qr{i}", [128, 256], BF16) for i in range(2)]
    kr = [sbt(p1, f"kr{i}", [128, 256], BF16) for i in range(2)]
    Qp = [pst(p1, f"Qp{i}", [128, 256]) for i in range(2)]
    Kp = [pst(p1, f"Kp{i}", [128, 256]) for i in range(2)]
    Vp = [pst(p1, f"Vp{i}", [128, 256]) for i in range(2)]
    ptB = [pst(p1, f"ptB{i}", [128, 512]) for i in range(2)]

    for nm, src, dst in [("wq", wq_s, wq_sb), ("wk", wk_s, wk_sb), ("wv", wv_s, wv_sb)]:
        A("G", (lambda s_, d_: lambda g: dma(
            g, d_[:, :].rearrange("p (c d) -> p c d", c=16),
            s_.ap().rearrange("(c p) d -> p c d", p=128)))(src, dst),
          inc=("qw", 16, ("w_" + nm,)))

    for wi, wsb_ in enumerate([wq_sb, wk_sb, wv_sb]):
        def f_cs1(wsb_=wsb_):
            def f(te):
                for k in range(16):
                    m = te.matmul(ptB[0][0:1, 0:256], onesb[:, :],
                                  wsb_[:, 256 * k:256 * (k + 1)],
                                  start=(k == 0), stop=(k == 15))
                return m
            return f
        w = [("qw", ("w_" + ["wq", "wk", "wv"][wi],)), ("vp", ("init",))]
        if wi > 0:
            w.append(("ap", ("cve", wi - 1)))
        A("T", f_cs1(), waits=w, inc=("tp", 1, ("cs1", wi)))

        def f_cs2(wi=wi):
            return lambda a: a.copy(colr[:, :], ptB[0][0:1, 0:256])
        A("A", f_cs2(), waits=[("tp", ("cs1", wi))], inc=("ap", 1, ("cs2", wi)))

        def f_cs3(wi=wi):
            return lambda te: te.matmul(ptB[0][:, 256:512], ones1b[:, :], colr[:, :],
                                        start=True, stop=True)
        A("T", f_cs3(), waits=[("ap", ("cs2", wi))], inc=("tp", 1, ("cs3", wi)))

        def f_cve(wi=wi):
            return lambda a: a.copy(cvb[wi][:, :], ptB[0][:, 256:512])
        A("A", f_cve(), waits=[("tp", ("cs3", wi))],
          inc=("ap", 1, ("cve", wi)))
    A("T", lambda te: te.nop(), waits=[("ap", ("cve", 2))], inc=("tp", 1, ("cvb",)))

    for t in range(NT + 1):
        tb = t % 2
        u = t - 1
        ub = u % 2

        if t < NT:
            def f_dx(t=t, tb=tb):
                return lambda e: dma(e, xf[tb][:, :], x_full[t * 128:(t + 1) * 128, :])
            w = []
            if t >= 2:
                w = [("vp", ("v1", t - 2)), ("ap", ("sq", t - 2))]
            A("S", f_dx(), waits=w, inc=("qx", 16, ("x", t)))

            def f_dxt(t=t, tb=tb):
                def f(e):
                    return dma(e, xtb[tb][:, :].rearrange("p (c tk) -> p c tk", c=16),
                               x_t[:, t * 128:(t + 1) * 128]
                               .rearrange("(c p) tk -> p c tk", p=128))
                return f
            w = [("tp", ("qkv", t - 2))] if t >= 2 else []
            A("G", f_dxt(), waits=w, inc=("qw", 16, ("xt", t)))

            def f_dc(t=t, tb=tb):
                return lambda e: dma(e, c2s[tb][:, :], c2_t[t * 128:(t + 1) * 128, :])
            def f_ds(t=t, tb=tb):
                return lambda e: dma(e, s2s[tb][:, :], s2_t[t * 128:(t + 1) * 128, :])
            w = [("vp", ("ropek", t - 2))] if t >= 2 else []
            A("G", f_dc(), waits=w, inc=("qw", 16, ("tabc", t)))
            A("G", f_ds(), waits=[], inc=("qw", 16, ("tab", t)))

            def f_sq(t=t, tb=tb):
                return lambda e: e.activation(dump[:, :], xf[tb][:, :], AF.Square,
                                              accum_out=ssq[tb][:, :])
            A("A", f_sq(), waits=[("qx", ("x", t))], inc=("ap", 1, ("sq", t)))

            def f_qkv(t=t, tb=tb):
                def f(te):
                    for (ps, wsb) in [(Qp[t % 2], wq_sb), (Kp[t % 2], wk_sb),
                                      (Vp[t % 2], wv_sb)]:
                        for k in range(16):
                            m = te.matmul(ps[:, :], xtb[tb][:, 128 * k:128 * (k + 1)],
                                          wsb[:, 256 * k:256 * (k + 1)],
                                          start=(k == 0), stop=(k == 15))
                    return m
                return f
            w = [("qw", ("xt", t)), ("qw", ("w_wq",)), ("qw", ("w_wk",)),
                 ("qw", ("w_wv",)), ("tp", ("cvb",))]
            if t >= 2:
                w.append(("ap", ("scl", t - 2)))
            A("T", f_qkv(), waits=w, inc=("tp", 1, ("qkv", t)))

        if u >= 0:
            def f_nrm(u=u, ub=ub):
                def f(v):
                    FV(v, v.tensor_scalar(nrmb[ub][:, :], meanb[ub][:, :],
                                          rstd[ub][:, :], -1.0,
                                          ALU.mult, ALU.mult))
                    return v.engine_nop()
                return f
            A("V", f_nrm(), waits=[("ap", ("rstd", u))], inc=("vp", 1, ("nrm", u)))

        if t < NT:
            def f_red(t=t, tb=tb):
                def f(v):
                    r_ = v.reduce_sum(srow[tb][:, :], xf[tb][:, :], AX)
                    if t == 0:
                        FV(v, r_)
                        r_ = v.engine_nop()
                    return r_
                return f
            A("V", f_red(), waits=[("qx", ("x", t))], inc=("vp", 1, None))

        if u >= 0:
            def f_scl(ps, dst, u=u, ub=ub):
                return lambda a: a.mul(dst[:, :], ps[:, :], rstd[ub][:, :])
            wA = [("tp", ("qkv", u)), ("ap", ("rstd", u))]
            if u >= 1:
                wA.append(("vp", ("ropek", u - 1)))
            A("A", f_scl(Qp[u % 2], qsf), waits=wA, inc=("ap", 1, None))
            A("A", f_scl(Kp[u % 2], ksf), waits=[], inc=("ap", 1, None))
            A("A", f_scl(Vp[u % 2], vsf), waits=[], inc=("ap", 1, ("scl", u)))

            def f_rope(sf, cv, dst, u=u, ub=ub):
                def f(v):
                    v.scalar_tensor_tensor(rt0[:, :], cv[:, :], nrmb[ub][:, :],
                                           sf[:, :], ALU.mult, ALU.add)
                    v.tensor_tensor(rt1[:, :], rt0[:, :], c2s[ub][:, :], ALU.mult)
                    pin = rt0[:, :].rearrange("p (h s i) -> p h s i", h=2, s=2)
                    sin_ = s2s[ub][:, :].rearrange("p (h s i) -> p h s i", h=2, s=2)
                    rto = rt2[:, :].rearrange("p (h s i) -> p h s i", h=2, s=2)
                    v.tensor_tensor(rto[:, :, 0, :], pin[:, :, 1, :],
                                    sin_[:, :, 0, :], ALU.mult)
                    FV(v, v.tensor_tensor(rto[:, :, 1, :], pin[:, :, 0, :],
                                          sin_[:, :, 1, :], ALU.mult))
                    return v.tensor_tensor(dst[:, :], rt1[:, :], rt2[:, :], ALU.add)
                return f
            wq_ = [("ap", ("scl", u)), ("qw", ("tab", u)), ("qw", ("tabc", u))]
            if u >= 2:
                wq_ += [("tp", ("trqk", u - 2)), ("ap", ("nsq", u - 2))]
            A("V", f_rope(qsf, cvb[0], qr[u % 2]), waits=wq_,
              inc=("vp", 1, ("ropeq", u)))

        if t < NT:
            def f_mean(t=t, tb=tb):
                def f(v):
                    FV(v, v.tensor_scalar(meanb[tb][:, :], srow[tb][:, :],
                                          1.0 / D, None, ALU.mult))
                    return v.engine_nop()
                return f
            A("V", f_mean(), waits=[], inc=("vp", 1, None))

        if u >= 0:
            A("V", f_rope(ksf, cvb[1], kr[u % 2]), waits=[],
              inc=("vp", 1, ("ropek", u)))

            def f_nsq(u=u):
                def f(a):
                    for src_, col in ((qr[u % 2], 0), (kr[u % 2], 1)):
                        for h in range(2):
                            m = a.activation(
                                dump[:, 128 * (2 * col + h):128 * (2 * col + h + 1)],
                                src_[:, 128 * h:128 * (h + 1)], AF.Square,
                                accum_out=nsqb[:, 2 * col + h:2 * col + h + 1])
                    return m
                return f
            A("A", f_nsq(), waits=[("vp", ("ropek", u))], inc=("ap", 1, ("nsq", u)))

        if t < NT:
            def f_m2(t=t, tb=tb):
                return lambda v: v.tensor_scalar(m2b[:, :], meanb[tb][:, :],
                                                 meanb[tb][:, :], None, ALU.mult)
            A("V", f_m2(), waits=[], inc=("vp", 1, None))

        if u >= 0:
            def f_vev(u=u, ub=ub):
                def f(v):
                    return v.scalar_tensor_tensor(Vb[:, u * 256:(u + 1) * 256],
                                                  cvb[2][:, :], nrmb[ub][:, :],
                                                  vsf[:, :], ALU.mult, ALU.add)
                return f
            A("V", f_vev(), waits=[("ap", ("scl", u))], inc=("vp", 1, ("vev", u)))

            if u + 1 < NT:
                def f_ve1(t=u + 1, tb=(u + 1) % 2):
                    return lambda v: v.tensor_scalar(vepsb[:, :], ssq[tb][:, :],
                                                     1.0 / D, None, ALU.mult)
                A("V", f_ve1(), waits=[("ap", ("sq", u + 1))], inc=("vp", 1, None))

            def f_nmax(u=u):
                def f(v):
                    v.tensor_tensor(nsqb[:, 4:5], nsqb[:, 0:1], nsqb[:, 1:2], ALU.max)
                    v.tensor_tensor(nsqb[:, 5:6], nsqb[:, 2:3], nsqb[:, 3:4], ALU.max)
                    v.tensor_tensor(mqr[:, 0:1], mqr[:, 0:1], nsqb[:, 4:5], ALU.max)
                    return v.tensor_tensor(mqr[:, 1:2], mqr[:, 1:2], nsqb[:, 5:6],
                                           ALU.max)
                return f
            A("V", f_nmax(), waits=[("ap", ("nsq", u)), ("vp", ("init",))],
              inc=("vp", 1, ("nmax", u)))

        if t == 0:
            def f_ve1z(v):
                r_ = v.tensor_scalar(vepsb[:, :], ssq[0][:, :], 1.0 / D,
                                     None, ALU.mult)
                FV(v, r_)
                return v.engine_nop()
            A("V", f_ve1z, waits=[("ap", ("sq", 0))], inc=("vp", 1, None))

        if u >= 0:
            def f_trqk(u=u):
                def f(te):
                    for h in range(2):
                        te.matmul(ptB[u % 2][:, 128 * h:128 * (h + 1)],
                                  qr[u % 2][:, 128 * h:128 * (h + 1)], identb[:, :],
                                  start=True, stop=True)
                    for h in range(2):
                        m = te.matmul(ptB[u % 2][:, 256 + 128 * h:256 + 128 * (h + 1)],
                                      kr[u % 2][:, 128 * h:128 * (h + 1)], identb[:, :],
                                      start=True, stop=True)
                    return m
                return f
            w = [("vp", ("ropeq", u)), ("vp", ("ropek", u)), ("qw", ("w_id",))]
            if u >= 2:
                w.append(("ap", ("qktev", u - 2)))
            A("T", f_trqk(), waits=w, inc=("tp", 1, ("trqk", u)))

        if t < NT:
            def f_ve2(t=t, tb=tb):
                return lambda v: v.scalar_tensor_tensor(vepsb[:, :], m2b[:, :], -1.0,
                                                        vepsb[:, :], ALU.mult, ALU.add)
            A("V", f_ve2(), waits=[], inc=("vp", 1, ("v1", t)))

            def f_rstd(t=t, tb=tb):
                def f(e):
                    FA(e, e.activation(lnvb[tb][:, :], vepsb[:, :], AF.Ln,
                                       bias=epsb[:, :]))
                    return e.activation(rstd[tb][:, :], lnvb[tb][:, :], AF.Exp,
                                        scale=-0.5)
                return f
            A("A", f_rstd(), waits=[("vp", ("v1", t))], inc=("ap", 1, ("rstd", t)))

        if u >= 0:
            def f_qktev(u=u):
                def f(a):
                    qdst = QT[:, :].rearrange("p (h tk) -> p h tk", h=2)[:, :, u * 128:(u + 1) * 128]
                    kdst = KT[:, :].rearrange("p (h tk) -> p h tk", h=2)[:, :, u * 128:(u + 1) * 128]
                    qsrc = ptB[u % 2][:, 0:256].rearrange("p (h x) -> p h x", h=2)
                    ksrc = ptB[u % 2][:, 256:512].rearrange("p (h x) -> p h x", h=2)
                    a.copy(qdst, qsrc)
                    return a.copy(kdst, ksrc)
                return f
            A("A", f_qktev(), waits=[("tp", ("trqk", u))], inc=("ap", 1, ("qktev", u)))

    A("V", lambda v: v.tensor_copy(mqrb[:, :], mqr[:, :]),
      waits=[("vp", ("nmax", NT - 1))], inc=("vp", 1, ("mqb",)))
    def f_ntr(te):
        te.matmul(ptB[0][0:1, 0:128], mqrb[:, 0:1], identb[:, :],
                  start=True, stop=True)
        return te.matmul(ptB[0][0:1, 128:256], mqrb[:, 1:2], identb[:, :],
                         start=True, stop=True)
    A("T", f_ntr, waits=[("vp", ("mqb",)), ("vp", ("qktev", NT - 2))],
      inc=("tp", 1, ("ntr",)))

    def f_nred(v):
        FV(v, v.reduce_max(mq2[0:1, 0:1], ptB[0][0:1, 0:128], AX))
        FV(v, v.reduce_max(mq2[0:1, 1:2], ptB[0][0:1, 128:256], AX))
        return v.tensor_tensor(sc1[:, 1:2], mq2[:, 0:1], mq2[:, 1:2], ALU.mult)
    A("V", f_nred, waits=[("tp", ("ntr",))], inc=("vp", 1, ("b1",)))

    def f_b2(e):
        FA(e, e.activation(sc1[:, 2:3], sc1[:, 1:2], AF.Ln))
        return e.activation(sc1[:, 4:5], sc1[:, 2:3], AF.Exp, scale=0.5)
    A("A", f_b2, waits=[("vp", ("b1",))], inc=("ap", 1, ("b2",)))

    def f_b3(v):
        FV(v, v.tensor_scalar(sc1[:, 3:4], sc1[:, 4:5], -1.002 * SCL, -1.0 * SCL,
                              ALU.mult, ALU.add))
        return v.tensor_copy(sc1b[:, :], sc1[:, 3:4])
    A("V", f_b3, waits=[("ap", ("b2",))], inc=("vp", 1, ("b3",)))

    A("T", lambda te: te.matmul(ptB[0][:, 0:1], ones1b[:, :], sc1b[:, :],
                                start=True, stop=True),
      waits=[("vp", ("b3",))], inc=("tp", 1, ("b4",)))

    A("V", lambda v: v.tensor_copy(bias_sb[:, :], ptB[0][:, 0:1]),
      waits=[("tp", ("b4",))], inc=("vp", 1, ("bias",)))

    p1.close()

    p2a = ExitStack()
    attnT = sbt(p2a, "attnT", [128, 2 * S], BF16)
    PTs = sbt(p2a, "PTs", [128, 2 * 2048], BF16)
    otsb = sbt(p2a, "otsb", [128, 512], BF16)
    arow = sbt(p2a, "arow", [128, 512], BF16)
    rrow = sbt(p2a, "rrow", [1, 512], F32)
    rrow2 = sbt(p2a, "rrow2", [1, 512], F32)
    rtl = sbt(p2a, "rtl", [128, 4], F32)
    stp = pst(p2a, "stp", [128, 2048])
    outp = [pst(p2a, f"outp{i}", [128, 512]) for i in range(2)]
    sump = pst(p2a, "sump", [1, 512])
    ptr1 = pst(p2a, "ptr1", [128, 512])

    quads = []
    groups = []
    for g in range(NB):
        for h in range(2):
            groups.append((g, h))
            for qd in range(g + 1):
                quads.append((g, h, qd))
    qdidx = {x: i for i, x in enumerate(quads)}
    gidx = {gh: i for i, gh in enumerate(groups)}

    pending_oav = []
    pending_dance = [None]

    def emit_dance(gh):
        g, h = gh
        gi = gidx[gh]

        def f_rtr(g=g, h=h, gi=gi):
            def f(te):
                for j in range(4):
                    te.matmul(ptr1[:, 128 * j:128 * (j + 1)],
                              otsb[:, 128 * j:128 * (j + 1)], identb[:, :],
                              start=True, stop=True)
                for j in range(4):
                    m = te.matmul(outp[gi % 2][:, 504 + j:505 + j],
                                  rrow2[:, 128 * j:128 * (j + 1)], onesf1[:, :],
                                  start=True, stop=True)
                return m
            return f
        A("T", f_rtr(), waits=[("vp", ("sev", g, h))], inc=("tp", 1, ("rtr", g, h)))

        def f_rsc(g=g, h=h, gi=gi):
            def f(v):
                FV(v, v.tensor_copy(rtl[:, :], outp[gi % 2][:, 504:508]))
                for j in range(4):
                    m = v.tensor_scalar(arow[:, 128 * j:128 * (j + 1)],
                                        ptr1[:, 128 * j:128 * (j + 1)],
                                        rtl[:, j:j + 1], None, ALU.mult)
                return m
            return f
        A("V", f_rsc(), waits=[("tp", ("rtr", g, h))], inc=("vp", 1, ("rsc", g, h)))

        def f_tb(g=g, h=h):
            def f(te):
                for j in range(4):
                    m = te.matmul(ptr1[:, 128 * j:128 * (j + 1)],
                                  arow[:, 128 * j:128 * (j + 1)], identb[:, :],
                                  start=True, stop=True)
                return m
            return f
        A("T", f_tb(), waits=[("vp", ("rsc", g, h))], inc=("tp", 1, ("tb", g, h)))

        def f_fin(g=g, h=h):
            return lambda v: v.tensor_copy(
                attnT[:, h * S + g * 512: h * S + (g + 1) * 512], ptr1[:, :])
        A("V", f_fin(), waits=[("tp", ("tb", g, h))], inc=("vp", 1, ("fin", g, h)))

    def flush_oav():
        fo, wo_, io_ = pending_oav.pop(0)
        A("T", fo, waits=wo_, inc=io_)
        if pending_dance[0] is not None:
            emit_dance(pending_dance[0])
            pending_dance[0] = None

    for g in range(NB):
        nq = g + 1
        for h in range(2):
            gi = gidx[(g, h)]
            for qd in range(nq):
                Q = qdidx[(g, h, qd)]
                masked = qd == nq - 1

                def f_st(g=g, h=h, qd=qd, Q=Q):
                    def f(te):
                        for s_ in range(4):
                            c = 4 * qd + s_
                            m = te.matmul(
                                stp[:, 512 * s_:512 * (s_ + 1)],
                                KT[:, h * S + c * 128: h * S + (c + 1) * 128],
                                QT[:, h * S + g * 512: h * S + (g + 1) * 512],
                                start=True, stop=True)
                        return m
                    return f
                w = []
                if Q >= 1:
                    w.append(("ap", ("exp", Q - 1)))
                if Q == 0:
                    w.append(("vp", ("qktev", NT - 1)))
                A("T", f_st(), waits=w, inc=("tp", 1, ("st", Q)))

                def f_exp(Q=Q):
                    return lambda e: e.activation(
                        PTs[:, (Q % 2) * 2048:(Q % 2 + 1) * 2048],
                        stp[:, :], AF.Exp, bias=bias_sb[:, :], scale=SCL)
                w = [("tp", ("st", Q)), ("vp", ("bias",))]
                if Q >= 2:
                    w.append(("tp", ("av", Q - 2)))
                A("A", f_exp(), waits=w, inc=("ap", 1, ("exp", Q)))

                if masked:
                    def f_msk(Q=Q):
                        def f(v):
                            sl = PTs[:, (Q % 2) * 2048:(Q % 2 + 1) * 2048]
                            return v.tensor_tensor(sl, sl, masks[:, :], ALU.mult)
                        return f
                    A("V", f_msk(), waits=[("ap", ("exp", Q)), ("qw", ("w_const",))],
                      inc=("vp", 1, ("msk", Q)))

                def f_oav(g=g, h=h, qd=qd, Q=Q, gi=gi, nq=nq):
                    def f(te):
                        for s_ in range(4):
                            c = 4 * qd + s_
                            pt = PTs[:, (Q % 2) * 2048 + 512 * s_:
                                     (Q % 2) * 2048 + 512 * (s_ + 1)]
                            te.matmul(sump[:, :], onesb[:, :], pt,
                                      start=(qd == 0 and s_ == 0),
                                      stop=(qd == nq - 1 and s_ == 3),
                                      skip_group_check=True)
                            m = te.matmul(
                                outp[gi % 2][:, :],
                                Vb[:, c * 256 + h * 128: c * 256 + (h + 1) * 128],
                                pt,
                                start=(qd == 0 and s_ == 0),
                                stop=(qd == nq - 1 and s_ == 3),
                                skip_group_check=True)
                        return m
                    return f
                w = [("vp", ("msk", Q))] if masked else [("ap", ("exp", Q))]
                if qd == 0:
                    if gi >= 1:
                        w.append(("vp", ("sev1", groups[gi - 1])))
                    if gi >= 2:
                        w.append(("vp", ("rsc", *groups[gi - 2])))
                pending_oav.append((f_oav(), w, ("tp", 1, ("av", Q))))
                if len(pending_oav) > 1:
                    flush_oav()

            while pending_oav:
                flush_oav()

            ql = qdidx[(g, h, nq - 1)]

            def f_seva(g=g, h=h):
                def f(v):
                    FV(v, v.tensor_copy(rrow[:, :], sump[:, :]))
                    return v.reciprocal(rrow2[:, :], rrow[:, :])
                return f
            A("V", f_seva(), waits=[("tp", ("av", ql))],
              inc=("vp", 1, ("sev1", (g, h))))

            def f_sevb(g=g, h=h, gi=gi):
                return lambda v: v.tensor_copy(otsb[:, :], outp[gi % 2][:, :])
            A("V", f_sevb(), waits=[], inc=("vp", 1, ("sev", g, h)))

            pending_dance[0] = (g, h)

    emit_dance(pending_dance[0])
    pending_dance[0] = None

    if _DEBUG:
        A("S", lambda e: dma(e, dbg_qt[:, :], QT[:, :]),
          waits=[("vp", ("qktev", NT - 1))], inc=("sto", 16, None))
        A("S", lambda e: dma(e, dbg_kt[:, :], KT[:, :]),
          waits=[], inc=("sto", 16, None))
        A("S", lambda e: dma(e, dbg_vb[:, :], Vb[:, :]),
          waits=[], inc=("sto", 16, None))
        A("S", lambda e: e.dma_start(out=dbg_bias[:, 0:1], in_=bias_sb[:, :],
                                     allow_slow_non_contiguous=True),
          waits=[("vp", ("bias",))], inc=("sto", 16, None))
        A("S", lambda e: dma(e, dbg_bias[0:1, 1:3], mq2[:, :]),
          waits=[], inc=("sto", 16, None))
        A("S", lambda e: dma(e, dbg_at[:, :], attnT[:, :]),
          waits=[("vp", ("fin", NB - 1, 1))], inc=("sto", 16, None))
    for j in range(NCORE):
        def f_stg(j=j):
            def f(e):
                src = attnT[:, :].rearrange("p (h tk) -> p h tk", h=2)[:, :, j * 1024:(j + 1) * 1024]
                dst = a2a_in[j * 256:(j + 1) * 256, :].rearrange("(h p) tk -> p h tk", p=128)
                return dma(e, dst, src)
            return f
        A("S", f_stg(), waits=[("vp", ("fin", NB - 1, 1))], inc=("qx", 16, ("stg", j)))

    for m_pre in range(2):
        for a_pre, srcw in ((0, eg_w), (1, eu_w)):
            def f_pre(m_pre=m_pre, a_pre=a_pre, srcw=srcw):
                def f(g):
                    return dma(g, guW[a_pre][m_pre][:, :]
                               .rearrange("p (c i) -> p c i", c=16),
                               srcw[0][:, m_pre * 128:(m_pre + 1) * 128]
                               .rearrange("(c p) i -> p c i", p=128))
                return f
            key = (("guw", ("e", 0), m_pre) if a_pre == 1
                   else ("guwg", ("e", 0), m_pre))
            A("G", f_pre(), waits=[], inc=("qw", 16, key))

    def f_cc(g):
        with allow_ici_all_to_all():
            return g.collective_compute(
                "AllToAll", ALU.bypass, replica_groups=[list(range(NCORE))],
                ins=[a2a_in.ap().opt()], outs=[a2a_out.ap().opt()])
    A("G", f_cc, waits=[("qx", ("stg", j_)) for j_ in range(NCORE)],
      inc=("cc", 1, ("cc",)))

    if _DEBUG:
        A("S", lambda e: dma(e, dbg_a2a[:, :], a2a_out.ap()),
          waits=[("cc", 1)], inc=("sto", 16, None))
    p2a.close()
    qkv_sc.close()

    p2b = ExitStack()
    acc = sbt(p2b, "acc", [128, NTO * D], F32)
    comb = sbt(p2b, "comb", [128, NTO * 8], F32)
    gwsb = sbt(p2b, "gwsb", [128, 16 * 8], F32)
    lgsb = sbt(p2b, "lgsb", [128, 8], F32)
    cm1 = sbt(p2b, "cm1", [128, 1], F32)
    cm1n = sbt(p2b, "cm1n", [128, 1], F32)
    cm2 = sbt(p2b, "cm2", [128, 1], F32)
    ceq = sbt(p2b, "ceq", [128, 8], F32)
    ce1 = sbt(p2b, "ce1", [128, 8], F32)
    cge = sbt(p2b, "cge", [128, 8], F32)
    cden = sbt(p2b, "cden", [128, 1], F32)
    cdenr = sbt(p2b, "cdenr", [128, 1], F32)
    tmp8 = sbt(p2b, "tmp8", [128, 8], F32)
    gu_sc = ExitStack()
    xnTo = sbt(gu_sc, "xnTo", [128, 16 * TOK], BF16)
    tsil = [sbt(gu_sc, f"tsil{i}", [128, 512], F32) for i in range(2)]

    ln2 = ExitStack()
    xnf = [sbt(ln2, f"xnf{i}", [128, D], F32) for i in range(2)]
    xnTf = sbt(ln2, "xnTf", [128, 16 * 128], F32)
    ptA2 = [pst(ln2, f"ptA2{i}", [128, 512]) for i in range(2)]
    lgps = pst(ln2, "lgps", [128, 8])
    ptF = pst(ln2, "ptF", [128, 512])

    A("S", lambda e: dma(e, gwsb[:, :].rearrange("p (c e) -> p c e", c=16),
                         gate_wTd.ap().rearrange("(c p) e -> p c e", p=128)),
      waits=[("qx", ("stg", j_)) for j_ in range(NCORE)], inc=("qx", 16, ("gw",)))
    A("S", lambda e: dma(e, acc[:, :].rearrange("p (t f) -> p t f", t=NTO),
                         x_own.ap().rearrange("(t p) f -> p t f", p=128)),
      waits=[("qx", ("stg", j_)) for j_ in range(NCORE)],
      inc=("qx", 16, ("acc0",)))

    for t in range(NTO):
        tb = t % 2
        gt = 1024 + t

        def f_dx2(t=t, tb=tb):
            return lambda e: dma(e, xf[tb][:, :], x_own[t * 128:(t + 1) * 128, :])
        if t >= 2:
            w = [("vp", ("apply", gt - 2)), ("ap", ("sq", gt - 2))]
        else:
            w = [("vp", ("v1", 62 + t)), ("ap", ("sq", 62 + t))]
        A("G", f_dx2(), waits=w, inc=("qw", 16, ("x", gt)))

        def f_sq2(t=t, tb=tb):
            return lambda e: e.activation(dump[:, :], xf[tb][:, :], AF.Square,
                                          accum_out=ssq[tb][:, :])
        A("A", f_sq2(), waits=[("qx", ("x", gt))], inc=("ap", 1, ("sq", gt)))

        def f_stats2(t=t, tb=tb):
            def f(v):
                FV(v, v.reduce_sum(srow[tb][:, :], xf[tb][:, :], AX))
                FV(v, v.tensor_scalar(meanb[tb][:, :], srow[tb][:, :], 1.0 / D,
                                      None, ALU.mult))
                FV(v, v.tensor_scalar(m2b[:, :], meanb[tb][:, :], meanb[tb][:, :],
                                      None, ALU.mult))
                FV(v, v.tensor_scalar(vepsb[:, :], ssq[tb][:, :], 1.0 / D, None,
                                      ALU.mult))
                FV(v, v.tensor_tensor(vepsb[:, :], vepsb[:, :], m2b[:, :],
                                      ALU.subtract))
                return v.tensor_scalar(vepsb[:, :], vepsb[:, :], EPS, None, ALU.add)
            return f
        A("V", f_stats2(), waits=[("qx", ("x", gt)), ("ap", ("sq", gt))],
          inc=("vp", 1, ("v1", gt)))

        def f_rstd2(t=t, tb=tb):
            def f(e):
                FA(e, e.activation(lnvb[tb][:, :], vepsb[:, :], AF.Ln))
                return e.activation(rstd[tb][:, :], lnvb[tb][:, :], AF.Exp, scale=-0.5)
            return f
        A("A", f_rstd2(), waits=[("vp", ("v1", gt))], inc=("ap", 1, ("rstd", gt)))

        def f_apply2(t=t, tb=tb):
            def f(v):
                v.tensor_scalar(xn[tb][:, :], xf[tb][:, :], meanb[tb][:, :],
                                rstd[tb][:, :], ALU.subtract, ALU.mult)
                return v.tensor_scalar(xnf[tb][:, :], xf[tb][:, :], meanb[tb][:, :],
                                       rstd[tb][:, :], ALU.subtract, ALU.mult)
            return f
        A("V", f_apply2(), waits=[("ap", ("rstd", gt))], inc=("vp", 1, ("apply", gt)))

        for j in range(4):
            gi = 1024 + t * 4 + j

            def f_tr2(t=t, tb=tb, j=j, gi=gi):
                def f(te):
                    for i in range(4):
                        c = 4 * j + i
                        m = te.matmul(ptA2[gi % 2][:, 128 * i:128 * (i + 1)],
                                      xn[tb][:, 128 * c:128 * (c + 1)],
                                      identb[:, :], start=True, stop=True)
                    return m
                return f
            w = [("vp", ("apply", gt))]
            if gi >= 1026:
                w.append(("vp", ("trev", gi - 2)))
            A("T", f_tr2(), waits=w, inc=("tp", 1, ("trb", gi)))

            def f_trev2(t=t, j=j, gi=gi):
                def f(v):
                    src = ptA2[gi % 2][:, :].rearrange("p (c i) -> p c i", c=4)
                    dst = xnTo[:, :].rearrange("p (c tk) -> p c tk", c=16)[
                        :, 4 * j:4 * j + 4, t * 128:(t + 1) * 128]
                    return v.tensor_copy(dst, src)
                return f
            A("V", f_trev2(), waits=[("tp", ("trb", gi))], inc=("vp", 1, ("trev", gi)))

        for j in range(4):
            def f_trfj(t=t, tb=tb, j=j):
                def f(te):
                    for i in range(4):
                        c = 4 * j + i
                        m = te.matmul(ptF[:, 128 * i:128 * (i + 1)],
                                      xnf[tb][:, 128 * c:128 * (c + 1)],
                                      identf[:, :], start=True, stop=True)
                    return m
                return f
            w = [("vp", ("apply", gt)), ("qw", ("w_idf",))]
            if j >= 1:
                w.append(("vp", ("trfe", gt, j - 1)))
            elif t >= 1:
                w.append(("tp", ("lg", t - 1)))
            A("T", f_trfj(), waits=w, inc=("tp", 1, ("trf", gt, j)))

            def f_trfe(t=t, j=j):
                def f(v):
                    return v.tensor_copy(xnTf[:, j * 512:(j + 1) * 512], ptF[:, :])
                return f
            w = [("tp", ("trf", gt, j))]
            if j == 0 and t >= 1:
                w.append(("tp", ("lg", t - 1)))
            A("V", f_trfe(), waits=w, inc=("vp", 1, ("trfe", gt, j)))

        def f_lg(t=t):
            def f(te):
                for k in range(16):
                    m = te.matmul(lgps[:, :], xnTf[:, 128 * k:128 * (k + 1)],
                                  gwsb[:, 8 * k:8 * (k + 1)],
                                  start=(k == 0), stop=(k == 15))
                return m
            return f
        w = [("vp", ("trfe", gt, 3)), ("qx", ("gw",))]
        if t >= 1:
            w.append(("vp", ("cmb1", t - 1)))
        A("T", f_lg(), waits=w, inc=("tp", 1, ("lg", t)))

        def f_comb1(t=t):
            def f(v):
                FV(v, v.tensor_copy(lgsb[:, :], lgps[:, :]))
                FV(v, v.reduce_max(cm1[:, :], lgsb[:, :], AX))
                v.tensor_scalar(cm1n[:, :], cm1[:, :], -1.0, None, ALU.mult)
                FV(v, v.tensor_scalar(ceq[:, :], lgsb[:, :], cm1[:, :], None,
                                      ALU.is_ge))
                FV(v, v.tensor_scalar(ceq[:, :], ceq[:, :], -1e30, None, ALU.mult))
                FV(v, v.tensor_tensor(ceq[:, :], ceq[:, :], lgsb[:, :], ALU.add))
                return v.reduce_max(cm2[:, :], ceq[:, :], AX)
            return f
        w = [("tp", ("lg", t))]
        if t >= 1:
            w.append(("ap", ("cmb2", t - 1)))
        A("V", f_comb1(), waits=w, inc=("vp", 1, ("cmb1", t)))

        def f_comb2(t=t):
            def f(e):
                e.activation(ce1[:, :], lgsb[:, :], AF.Exp, bias=cm1n[:, :])
                return e.activation(cden[:, :], cm2[:, :], AF.Exp, bias=cm1n[:, :])
            return f
        A("A", f_comb2(), waits=[("vp", ("cmb1", t))], inc=("ap", 1, ("cmb2", t)))

        def f_comb3(t=t):
            def f(v):
                FV(v, v.tensor_scalar(cden[:, :], cden[:, :], 1.0, None, ALU.add))
                FV(v, v.reciprocal(cdenr[:, :], cden[:, :]))
                FV(v, v.tensor_scalar(cge[:, :], lgsb[:, :], cm2[:, :], None,
                                      ALU.is_ge))
                FV(v, v.tensor_scalar(tmp8[:, :], ce1[:, :], cdenr[:, :], None,
                                      ALU.mult))
                return v.tensor_tensor(comb[:, t * 8:(t + 1) * 8], tmp8[:, :],
                                       cge[:, :], ALU.mult)
            return f
        A("V", f_comb3(), waits=[("ap", ("cmb2", t))], inc=("vp", 1, ("cmb3", t)))

    if _DEBUG:
        A("S", lambda e: dma(e, dbg_cmb[:, :], comb[:, :]),
          waits=[("vp", ("cmb3", NTO - 1))], inc=("sto", 16, None))
        A("S", lambda e: dma(e, dbg_xnto[:, :], xnTo[:, :]),
          waits=[("vp", ("trev", 1024 + (NTO - 1) * 4 + 3))], inc=("sto", 16, None))
    ln2.close()
    GU = [pst(p2b, f"GU{i}", [128, 512]) for i in range(4)]
    EPm = [pst(p2b, f"EPm{i}", [128, 512]) for i in range(4)]

    exp_sc = ExitStack()
    actT = sbt(exp_sc, "actT", [128, 11 * TOK], BF16)
    dwn = [sbt(exp_sc, f"dwn{i}", [128, 11 * 512], BF16, side="right") for i in range(2)]

    gu_hist = []
    sil_hist = []
    ep_hist = []

    def emit_gu(tag, nm, src_g, src_u, actdst, e_idx, first_extra_wait,
                preloaded=0):
        for m in range(nm):
            li = len(gu_hist)
            gu_hist.append((tag, m))
            mb = li % 3
            skip_load = m < preloaded

            def f_wg(m=m, mb=mb):
                def f(g):
                    sl = src_g.ap() if e_idx is None else src_g[e_idx]
                    return dma(g, guW[0][mb][:, :].rearrange("p (c i) -> p c i", c=16),
                               sl[:, m * 128:(m + 1) * 128]
                               .rearrange("(c p) i -> p c i", p=128))
                return f

            def f_wu(m=m, mb=mb):
                def f(g):
                    sl = src_u.ap() if e_idx is None else src_u[e_idx]
                    return dma(g, guW[1][mb][:, :].rearrange("p (c i) -> p c i", c=16),
                               sl[:, m * 128:(m + 1) * 128]
                               .rearrange("(c p) i -> p c i", p=128))
                return f
            if not skip_load:
                w = []
                if li >= 3:
                    pt_, pm_ = gu_hist[li - 3]
                    w.append(("vp", ("gact", pt_, pm_, 1)))
                A("G", f_wg(), waits=w, inc=("qw", 16, ("guwg", tag, m)))
                A("G", f_wu(), waits=[], inc=("qw", 16, ("guw", tag, m)))

            for hh in range(2):
                def f_gu(m=m, mb=mb, hh=hh):
                    def f(te):
                        for k in range(16):
                            te.matmul(GU[hh][:, :],
                                      guW[0][mb][:, 128 * k:128 * (k + 1)],
                                      xnTo[:, k * TOK + hh * 512: k * TOK + hh * 512 + 512],
                                      start=(k == 0), stop=(k == 15))
                        for k in range(16):
                            m_ = te.matmul(GU[2 + hh][:, :],
                                           guW[1][mb][:, 128 * k:128 * (k + 1)],
                                           xnTo[:, k * TOK + hh * 512: k * TOK + hh * 512 + 512],
                                           start=(k == 0), stop=(k == 15))
                        return m_
                    return f
                w = [("qw", ("guw", tag, m)), ("qw", ("guwg", tag, m))]
                if m == 0 and hh == 0 and first_extra_wait:
                    w += first_extra_wait
                if m >= 1:
                    w.append(("vp", ("gact", tag, m - 1, hh)))
                A("T", f_gu(), waits=w, inc=("tp", 1, ("gu", tag, m, hh)))

                sli = len(sil_hist)
                sil_hist.append((tag, m, hh))

                def f_sil(m=m, hh=hh, sli=sli):
                    return lambda e: e.activation(tsil[sli % 2][:, :], GU[hh][:, :],
                                                  AF.Silu)
                w = [("tp", ("gu", tag, m, hh))]
                if sli >= 2:
                    w.append(("vp", ("gact", *sil_hist[sli - 2])))
                A("A", f_sil(), waits=w, inc=("ap", 1, ("sil", tag, m, hh)))

                def f_gact(m=m, hh=hh, sli=sli, actdst=actdst):
                    def f(v):
                        return v.tensor_tensor(
                            actdst[:, m * TOK + hh * 512: m * TOK + hh * 512 + 512],
                            tsil[sli % 2][:, :], GU[2 + hh][:, :], ALU.mult)
                    return f
                A("V", f_gact(), waits=[("ap", ("sil", tag, m, hh))],
                  inc=("vp", 1, ("gact", tag, m, hh)))

    def ep_prev_mark(kind):
        if not ep_hist:
            return None
        u = ep_hist[-1]
        if u[0] == "dn":
            return ("dev", u[1], u[2], u[3], 3)
        return ("swev", u[1], u[2], 3)

    def prefetch_guw(tag, src_g, src_u, e_idx):
        for m in range(2):
            li_pred = len(gu_hist) + m
            mb = li_pred % 3
            for a_, srcw in ((0, src_g), (1, src_u)):
                def f_pre(m=m, mb=mb, a_=a_, srcw=srcw, e_idx=e_idx):
                    def f(g):
                        sl = srcw.ap() if e_idx is None else srcw[e_idx]
                        return dma(g, guW[a_][mb][:, :]
                                   .rearrange("p (c i) -> p c i", c=16),
                                   sl[:, m * 128:(m + 1) * 128]
                                   .rearrange("(c p) i -> p c i", p=128))
                    return f
                w = [("vp", ("gact", *gu_hist[li_pred - 3], 1))]
                key = ("guw", tag, m) if a_ == 1 else ("guwg", tag, m)
                A("G", f_pre(), waits=w, inc=("qw", 16, key))

    dwn_hist = []
    for e in range(8):
        emit_gu(("e", e), 11, eg_w, eu_w, actT, e,
                [("vp", ("cmb3", NTO - 1)), ("vp", ("trev", 1024 + (NTO - 1) * 4 + 3))]
                if e == 0 else None, preloaded=2)
        if _DEBUG and e == 0:
            A("S", lambda e_: dma(e_, dbg_act[:, :], actT[:, :]),
              waits=[("vp", ("gact", ("e", 0), 10, 1))], inc=("sto", 16, None))
        for n in range(4):
            dli = len(dwn_hist)
            dwn_hist.append((e, n))
            db = dli % 2

            def f_dw(e=e, n=n, db=db):
                def f(g):
                    return dma(g, dwn[db][:, :].rearrange("p (c f) -> p c f", c=11),
                               ed_w[e][:, n * 512:(n + 1) * 512]
                               .rearrange("(c p) f -> p c f", p=128))
                return f
            w = []
            if dli == 0:
                w.append(("tp", ("lg", NTO - 1)))
            if dli >= 2:
                pe_, pn_ = dwn_hist[dli - 2]
                w.append(("tp", ("dn", pe_, pn_, 1)))
            A("G", f_dw(), waits=w, inc=("qw", 16, ("dw", e, n)))

            for half in range(2):
                pm = ep_prev_mark("dn")
                ep_hist.append(("dn", e, n, half))

                def f_down(e=e, n=n, db=db, half=half):
                    def f(te):
                        for c in range(11):
                            for ti in range(4):
                                tile = half * 4 + ti
                                m = te.matmul(
                                    EPm[ti][:, :],
                                    actT[:, c * TOK + tile * 128: c * TOK + (tile + 1) * 128],
                                    dwn[db][:, c * 512:(c + 1) * 512],
                                    start=(c == 0), stop=(c == 10),
                                    skip_group_check=True)
                        return m
                    return f
                w = [("qw", ("dw", e, n)), ("vp", ("gact", ("e", e), 10, 1))]
                if pm is not None:
                    w.append(("vp", pm))
                A("T", f_down(), waits=w, inc=("tp", 1, ("dn", e, n, half)))

                for ti in range(4):
                    def f_dev(e=e, n=n, half=half, ti=ti):
                        def f(v):
                            tile = half * 4 + ti
                            sl = acc[:, tile * D + n * 512: tile * D + n * 512 + 512]
                            return v.scalar_tensor_tensor(
                                sl, EPm[ti][:, :],
                                comb[:, tile * 8 + e: tile * 8 + e + 1],
                                sl, ALU.mult, ALU.add)
                        return f
                    w = [("tp", ("dn", e, n, half)), ("qx", ("acc0",))]
                    A("V", f_dev(), waits=w, inc=("vp", 1, ("dev", e, n, half, ti)))

            if n == 1:
                if e < 7:
                    prefetch_guw(("e", e + 1), eg_w, eu_w, e + 1)
                else:
                    prefetch_guw(("s", 0), sg_w, su_w, None)

    exp_sc.close()
    sh_sc = ExitStack()
    actsh = sbt(sh_sc, "actsh", [128, 22 * TOK], BF16, side="right")

    emit_gu(("s", 0), 22, sg_w, su_w, actsh, None, None, preloaded=2)

    gu_sc.close()
    sw_sc = ExitStack()
    sdw = sbt(sw_sc, "sdw", [128, 22 * 512], BF16)
    wow = sbt(sw_sc, "wow", [128, 16 * 512], BF16)
    maw = [sbt(sw_sc, f"maw{i}", [128, TOK], BF16) for i in range(3)]

    maw_hist = []
    for n in range(4):
        def f_sdw(n=n):
            def f(g):
                return dma(g, sdw[:, :].rearrange("p (c f) -> p c f", c=22),
                           sd_w[:, n * 512:(n + 1) * 512]
                           .rearrange("(c p) f -> p c f", p=128))
            return f
        w = ([("tp", ("wo", n - 1, 1, 15))] if n >= 1
             else [("tp", ("dn", 7, 3, 1)), ("tp", ("gu", ("s", 0), 21, 1))])
        A("G", f_sdw(), waits=w, inc=("qw", 16, ("sdw", n)))

        def f_wow(n=n):
            def f(g):
                return dma(g, wow[:, :].rearrange("p (c f) -> p c f", c=16),
                           wo_full[:, n * 512:(n + 1) * 512]
                           .rearrange("(c p) f -> p c f", p=128))
            return f
        A("G", f_wow(), waits=[], inc=("qw", 16, ("wow", n)))

        for half in range(2):
            pm = ep_prev_mark("sw")
            ep_hist.append(("sw", n, half))

            def f_sd(n=n, half=half):
                def f(te):
                    for c in range(22):
                        for ti in range(4):
                            tile = half * 4 + ti
                            m = te.matmul(
                                EPm[ti][:, :],
                                actsh[:, c * TOK + tile * 128: c * TOK + (tile + 1) * 128],
                                sdw[:, c * 512:(c + 1) * 512],
                                start=(c == 0), stop=False,
                                skip_group_check=True)
                    return m
                return f
            w = [("qw", ("sdw", n)), ("vp", ("gact", ("s", 0), 21, 1))]
            if pm is not None:
                w.append(("vp", pm))
            A("T", f_sd(), waits=w, inc=("tp", 1, ("sd", n, half)))

            for jj in range(16):
                mli = len(maw_hist)
                maw_hist.append((n, half, jj))

                def f_maw(n=n, half=half, jj=jj, mli=mli):
                    return lambda e: dma(e, maw[mli % 3][:, :],
                                         a2a_out[jj * 128:(jj + 1) * 128, :])
                w = [("cc", 1)]
                if mli < 3:
                    w += [("tp", ("dn", 7, 3, 1)), ("tp", ("gu", ("s", 0), 21, 1))]
                if mli >= 3:
                    w.append(("tp", ("wo", *maw_hist[mli - 3])))
                A("S", f_maw(), waits=w, inc=("qx", 16, ("maw", n, half, jj)))

                def f_wo(n=n, half=half, jj=jj, mli=mli):
                    def f(te):
                        for ti in range(4):
                            tile = half * 4 + ti
                            m = te.matmul(
                                EPm[ti][:, :],
                                maw[mli % 3][:, tile * 128:(tile + 1) * 128],
                                wow[:, jj * 512:(jj + 1) * 512],
                                start=False, stop=(jj == 15),
                                skip_group_check=True)
                        return m
                    return f
                w = [("qx", ("maw", n, half, jj)), ("qw", ("wow", n)),
                     ("tp", ("sd", n, half))]
                A("T", f_wo(), waits=w, inc=("tp", 1, ("wo", n, half, jj)))

            for ti in range(4):
                def f_swev(n=n, half=half, ti=ti):
                    def f(v):
                        tile = half * 4 + ti
                        sl = acc[:, tile * D + n * 512: tile * D + n * 512 + 512]
                        return v.tensor_tensor(sl, sl, EPm[ti][:, :], ALU.add)
                    return f
                A("V", f_swev(), waits=[("tp", ("wo", n, half, 15))],
                  inc=("vp", 1, ("swev", n, half, ti)))

    def f_out(e):
        return dma(e, out_d.ap().rearrange("(t p) f -> p t f", p=128),
                   acc[:, :].rearrange("p (t f) -> p t f", t=NTO))
    A("S", f_out, waits=[("vp", ("swev", 3, 1, 3))], inc=("sto", 16, ("out",)))

    A("S", lambda e: e.nop(), waits=[("sto", ("out",))], inc=None)

    rot = {"qx": 0, "qw": 0}
    events2 = []
    for eng, fn, waits, inc in ev.events:
        if inc is not None and inc[0] in rot:
            sem, amt, key = inc
            phys = f"{sem}{rot[sem] % NROT}"
            rot[sem] += 1
            inc = (phys, amt, key)
        events2.append((eng, fn, waits, inc))
    ev.events = events2

    cnt = defaultdict(int)
    marks = {}
    for eng, fn, waits, inc in ev.events:
        if inc is not None:
            sem, amt, key = inc
            cnt[sem] += amt
            if key is not None:
                assert key not in marks, key
                marks[key] = (sem, cnt[sem])

    def resolve(w):
        sem, key = w
        if isinstance(key, int):
            return sem, key
        msem, val = marks[key]
        return msem, val

    with nc.Block() as block:
        def emit_for(eng_name, selector):
            def body(e):
                hw = defaultdict(int)
                emitted = defaultdict(int)
                for eng, fn, waits, inc in ev.events:
                    if eng != eng_name:
                        if inc is not None and inc[0][:2] in ("qx", "qw"):
                            emitted[inc[0]] += inc[1]
                        continue
                    for w_ in waits:
                        sem, val = resolve(w_)
                        if val > hw[sem]:
                            e.wait_ge(semh[sem], val)
                            hw[sem] = val
                    if inc is not None and inc[0][:2] in ("qx", "qw"):
                        sem = inc[0]
                        prev = emitted[sem]
                        if prev > 0 and prev > hw[sem]:
                            e.wait_ge(semh[sem], prev)
                            hw[sem] = prev
                        emitted[sem] += inc[1]
                    inst = fn(e)
                    if inc is not None:
                        sem, amt, _ = inc
                        inst.then_inc(semh[sem], amt)
            selector(body)

        emit_for("S", block.sync)
        emit_for("G", block.gpsimd)
        emit_for("T", block.tensor)
        emit_for("A", block.scalar)
        emit_for("V", block.vector)

    sw_sc.close()
    sh_sc.close()
    p2b.close()
    root.close()
    return nc


def _rope_tables():
    exps = np.arange(0, 128, 2, dtype=np.float32) / np.float32(128.0)
    inv_freq = (np.float32(1.0) / (np.float32(10000.0) ** exps)).astype(np.float32)
    pos = np.arange(S, dtype=np.float32)
    ang = (pos[:, None] * inv_freq[None, :]).astype(np.float32)
    cos = np.cos(ang.astype(np.float64)).astype(np.float32)
    sin = np.sin(ang.astype(np.float64)).astype(np.float32)
    c1 = np.concatenate([cos, cos], axis=1)
    s1 = np.concatenate([-sin, sin], axis=1)
    C2 = np.ascontiguousarray(np.concatenate([c1, c1], axis=1))
    S2 = np.ascontiguousarray(np.concatenate([s1, s1], axis=1))
    return C2, S2


def _mask_tiles():
    kk = np.arange(128)[:, None]
    qq_ = np.arange(512)[None, :]
    tiles = [(128 * m + kk <= qq_).astype(np.float32) for m in range(4)]
    return np.ascontiguousarray(np.concatenate(tiles, axis=1))


def kernel(L0_attn_x, L0_attn_wq, L0_attn_wk, L0_attn_wv, L0_attn_wo,
           L0_ffn_gate_w, L0_ffn_expert_gate, L0_ffn_expert_up,
           L0_ffn_expert_down, L0_ffn_shared_gate, L0_ffn_shared_up,
           L0_ffn_shared_down):
    x = np.ascontiguousarray(np.asarray(L0_attn_x, dtype=np.float32).reshape(S, D))
    wq = np.asarray(L0_attn_wq, dtype=np.float32)
    wk = np.asarray(L0_attn_wk, dtype=np.float32)
    wv = np.asarray(L0_attn_wv, dtype=np.float32)
    wo = np.ascontiguousarray(np.asarray(L0_attn_wo, dtype=np.float32))
    gate_w = np.asarray(L0_ffn_gate_w, dtype=np.float32)
    eg = np.asarray(L0_ffn_expert_gate, dtype=np.float32)
    eu = np.asarray(L0_ffn_expert_up, dtype=np.float32)
    edn = np.asarray(L0_ffn_expert_down, dtype=np.float32)
    sg = np.asarray(L0_ffn_shared_gate, dtype=np.float32)
    su = np.asarray(L0_ffn_shared_up, dtype=np.float32)
    sdn = np.asarray(L0_ffn_shared_down, dtype=np.float32)

    C2, S2 = _rope_tables()
    shared = dict(
        x_full=x,
        x_t=np.ascontiguousarray(x.T),
        wo_full=wo,
        c2_t=C2, s2_t=S2,
        ident_in=np.eye(128, dtype=np.float32),
        mask_in=_mask_tiles(),
        gate_wTd=np.ascontiguousarray(gate_w.T),
        eg_w=np.ascontiguousarray(np.transpose(eg, (0, 2, 1))),
        eu_w=np.ascontiguousarray(np.transpose(eu, (0, 2, 1))),
        ed_w=np.ascontiguousarray(np.transpose(edn, (0, 2, 1))),
        sg_w=np.ascontiguousarray(sg.T),
        su_w=np.ascontiguousarray(su.T),
        sd_w=np.ascontiguousarray(sdn.T),
    )
    in_maps = []
    for c in range(NCORE):
        m = dict(shared)
        m["x_own"] = np.ascontiguousarray(x[c * TOK:(c + 1) * TOK, :])
        m["wq_s"] = np.ascontiguousarray(wq[:, c * 256:(c + 1) * 256])
        m["wk_s"] = np.ascontiguousarray(wk[:, c * 256:(c + 1) * 256])
        m["wv_s"] = np.ascontiguousarray(wv[:, c * 256:(c + 1) * 256])
        in_maps.append(m)

    nc = _build()
    res = run_bass_kernel_spmd(nc, in_maps, core_ids=list(range(NCORE)))
    out = np.concatenate([res.results[c]["out"] for c in range(NCORE)], axis=0)
    return out.reshape(1, S, D)


# version 18
# speedup vs baseline: 1.0141x; 1.0100x over previous
import math
from collections import defaultdict
from contextlib import ExitStack

import numpy as np

import concourse.bass as bass
import concourse.mybir as mybir
from concourse.bass_utils import run_bass_kernel_spmd
from concourse.aot_env import allow_ici_all_to_all

F32 = mybir.dt.float32
BF16 = mybir.dt.bfloat16
AF = mybir.ActivationFunctionType
ALU = mybir.AluOpType
AX = mybir.AxisListType.X

S = 8192
D = 2048
NCORE = 8
TOK = S // NCORE
NT = S // 128
NB = S // 512
NTO = TOK // 128
EI = 1408
ISH = 2816
SCL = 1.0 / math.sqrt(128.0)
EPS = 1e-5


_DEBUG = False




class EV:
    def __init__(self):
        self.events = []

    def add(self, eng, fn, waits=(), inc=None):
        self.events.append((eng, fn, tuple(waits), inc))


def _build():
    nc = bass.Bass("TRN2", target_bir_lowering=False)

    x_full = nc.dram_tensor("x_full", [S, D], F32, kind="ExternalInput")
    x_own = nc.dram_tensor("x_own", [TOK, D], F32, kind="ExternalInput")
    x_t = nc.dram_tensor("x_t", [D, S], F32, kind="ExternalInput")
    wq_s = nc.dram_tensor("wq_s", [D, 256], F32, kind="ExternalInput")
    wk_s = nc.dram_tensor("wk_s", [D, 256], F32, kind="ExternalInput")
    wv_s = nc.dram_tensor("wv_s", [D, 256], F32, kind="ExternalInput")
    wo_full = nc.dram_tensor("wo_full", [D, D], F32, kind="ExternalInput")
    c2_t = nc.dram_tensor("c2_t", [S, 256], F32, kind="ExternalInput")
    s2_t = nc.dram_tensor("s2_t", [S, 256], F32, kind="ExternalInput")
    ident_in = nc.dram_tensor("ident_in", [128, 128], F32, kind="ExternalInput")
    mask_in = nc.dram_tensor("mask_in", [128, 4 * 512], F32, kind="ExternalInput")
    gate_wTd = nc.dram_tensor("gate_wTd", [D, 8], F32, kind="ExternalInput")
    eg_w = nc.dram_tensor("eg_w", [8, D, EI], F32, kind="ExternalInput")
    eu_w = nc.dram_tensor("eu_w", [8, D, EI], F32, kind="ExternalInput")
    ed_w = nc.dram_tensor("ed_w", [8, EI, D], F32, kind="ExternalInput")
    sg_w = nc.dram_tensor("sg_w", [D, ISH], F32, kind="ExternalInput")
    su_w = nc.dram_tensor("su_w", [D, ISH], F32, kind="ExternalInput")
    sd_w = nc.dram_tensor("sd_w", [ISH, D], F32, kind="ExternalInput")
    out_d = nc.dram_tensor("out", [TOK, D], F32, kind="ExternalOutput")

    a2a_in = nc.dram_tensor("a2a_in", [D, TOK], BF16, kind="Internal")
    a2a_out = nc.dram_tensor("a2a_out", [D, TOK], BF16, kind="Internal")
    if _DEBUG:
        dbg_qt = nc.dram_tensor("dbg_qt", [128, 2 * S], BF16, kind="ExternalOutput")
        dbg_kt = nc.dram_tensor("dbg_kt", [128, 2 * S], BF16, kind="ExternalOutput")
        dbg_vb = nc.dram_tensor("dbg_vb", [128, NT * 256], BF16, kind="ExternalOutput")
        dbg_bias = nc.dram_tensor("dbg_bias", [128, 4], F32, kind="ExternalOutput")
        dbg_at = nc.dram_tensor("dbg_at", [128, 2 * S], BF16, kind="ExternalOutput")
        dbg_a2a = nc.dram_tensor("dbg_a2a", [D, TOK], BF16, kind="ExternalOutput")
        dbg_cmb = nc.dram_tensor("dbg_cmb", [128, NTO * 8], F32, kind="ExternalOutput")
        dbg_xnto = nc.dram_tensor("dbg_xnto", [128, 16 * TOK], BF16, kind="ExternalOutput")
        dbg_act = nc.dram_tensor("dbg_act", [128, 11 * TOK], BF16, kind="ExternalOutput")

    root = ExitStack()
    def sbt(st, n, s, d, side=None):
        return st.enter_context(nc.sbuf_tensor(n, s, d, side=side))
    def pst(st, n, s, d=F32):
        return st.enter_context(nc.psum_tensor(n, s, d))

    identb = sbt(root, "identb", [128, 128], BF16)
    identf = sbt(root, "identf", [128, 128], F32)
    onesb = sbt(root, "onesb", [128, 1], BF16)
    ones1b = sbt(root, "ones1b", [1, 128], BF16)
    onesf1 = sbt(root, "onesf1", [1, 1], F32)
    bias_sb = sbt(root, "bias_sb", [128, 1], F32)
    mq2 = sbt(root, "mq2", [1, 2], F32)
    nsqb = sbt(root, "nsqb", [128, 6], F32)
    mqr = sbt(root, "mqr", [128, 2], F32)
    mqrb = sbt(root, "mqrb", [128, 2], BF16)
    sc1 = sbt(root, "sc1", [1, 6], F32)
    sc1b = sbt(root, "sc1b", [1, 1], BF16)
    xf = [sbt(root, f"xf{i}", [128, D], F32) for i in range(2)]
    xn = [sbt(root, f"xn{i}", [128, D], BF16) for i in range(2)]
    dump = sbt(root, "dump", [128, D], BF16)
    srow = [sbt(root, f"srow{i}", [128, 1], F32) for i in range(2)]
    ssq = [sbt(root, f"ssq{i}", [128, 1], F32) for i in range(2)]
    meanb = [sbt(root, f"meanb{i}", [128, 1], F32) for i in range(2)]
    m2b = sbt(root, "m2b", [128, 1], F32)
    vepsb = sbt(root, "vepsb", [128, 1], F32)
    rstd = [sbt(root, f"rstd{i}", [128, 1], F32) for i in range(2)]
    lnvb = [sbt(root, f"lnvb{i}", [128, 1], F32) for i in range(2)]
    epsb = sbt(root, "epsb", [128, 1], F32)
    guW = [[sbt(root, f"guW{a}{i}", [128, 16 * 128], BF16) for i in range(3)]
           for a in range(2)]
    qkv_sc = ExitStack()
    masks = sbt(qkv_sc, "masks", [128, 4 * 512], BF16)
    QT = sbt(qkv_sc, "QT", [128, 2 * S], BF16)
    KT = sbt(qkv_sc, "KT", [128, 2 * S], BF16)
    Vb = sbt(qkv_sc, "Vb", [128, NT * 256], BF16)

    semh = {}
    for s_ in ["tp", "ap", "vp", "pp", "cc", "sto", "vf", "af", "pf"]:
        semh[s_] = root.enter_context(nc.semaphore(s_))
    NROT = 8
    for s_ in ["qx", "qw"]:
        for i_ in range(NROT):
            semh[f"{s_}{i_}"] = root.enter_context(nc.semaphore(f"{s_}{i_}"))

    _fcnt = {"vf": 0, "af": 0, "pf": 0}

    def FV(v, inst):
        _fcnt["vf"] += 1
        inst.then_inc(semh["vf"], 1)
        v.wait_ge(semh["vf"], _fcnt["vf"])

    def FA(a, inst):
        _fcnt["af"] += 1
        inst.then_inc(semh["af"], 1)
        a.wait_ge(semh["af"], _fcnt["af"])

    def FP(p, inst):
        _fcnt["pf"] += 1
        inst.then_inc(semh["pf"], 1)
        p.wait_ge(semh["pf"], _fcnt["pf"])

    ev = EV()
    A = ev.add

    def dma(e, out, in_):
        return e.dma_start(out=out, in_=in_)

    A("G", lambda g: dma(g, identb[:, :], ident_in[:, :]), inc=("qw", 16, ("w_id",)))
    A("G", lambda g: dma(g, identf[:, :], ident_in[:, :]), inc=("qw", 16, ("w_idf",)))
    A("G", lambda g: dma(g, masks[:, :], mask_in[:, :]), inc=("qw", 16, ("w_const",)))
    A("V", lambda v: v.memset(onesb[:, :], 1.0), inc=("vp", 1, None))
    A("V", lambda v: v.memset(ones1b[:, :], 1.0), inc=("vp", 1, None))
    A("V", lambda v: v.memset(onesf1[:, :], 1.0), inc=("vp", 1, None))
    A("V", lambda v: v.memset(mq2[:, :], 0.0), inc=("vp", 1, None))
    A("V", lambda v: v.memset(mqr[:, :], 0.0), inc=("vp", 1, None))
    A("V", lambda v: v.memset(epsb[:, :], EPS), inc=("vp", 1, ("init",)))

    p1 = ExitStack()
    wq_sb = sbt(p1, "wq_sb", [128, 16 * 256], BF16)
    wk_sb = sbt(p1, "wk_sb", [128, 16 * 256], BF16)
    wv_sb = sbt(p1, "wv_sb", [128, 16 * 256], BF16)
    xtb = [sbt(p1, f"xtb{i}", [128, 16 * 128], BF16) for i in range(2)]
    c2s = [sbt(p1, f"c2s{i}", [128, 256], BF16) for i in range(2)]
    s2s = [sbt(p1, f"s2s{i}", [128, 256], BF16) for i in range(2)]
    rt0 = sbt(p1, "rt0", [128, 256], F32)
    rt1 = sbt(p1, "rt1", [128, 256], F32)
    rt2 = sbt(p1, "rt2", [128, 256], F32)
    qsf = sbt(p1, "qsf", [128, 256], F32)
    ksf = sbt(p1, "ksf", [128, 256], F32)
    vsf = sbt(p1, "vsf", [128, 256], F32)
    nrmb = [sbt(p1, f"nrmb{i}", [128, 1], F32) for i in range(2)]
    cvb = [sbt(p1, f"cvb{i}", [128, 256], BF16) for i in range(3)]
    colr = sbt(p1, "colr", [1, 256], BF16)
    qr = [sbt(p1, f"qr{i}", [128, 256], BF16) for i in range(2)]
    kr = [sbt(p1, f"kr{i}", [128, 256], BF16) for i in range(2)]
    Qp = [pst(p1, f"Qp{i}", [128, 256]) for i in range(2)]
    Kp = [pst(p1, f"Kp{i}", [128, 256]) for i in range(2)]
    Vp = [pst(p1, f"Vp{i}", [128, 256]) for i in range(2)]
    ptB = [pst(p1, f"ptB{i}", [128, 512]) for i in range(2)]

    for nm, src, dst in [("wq", wq_s, wq_sb), ("wk", wk_s, wk_sb), ("wv", wv_s, wv_sb)]:
        A("G", (lambda s_, d_: lambda g: dma(
            g, d_[:, :].rearrange("p (c d) -> p c d", c=16),
            s_.ap().rearrange("(c p) d -> p c d", p=128)))(src, dst),
          inc=("qw", 16, ("w_" + nm,)))

    for wi, wsb_ in enumerate([wq_sb, wk_sb, wv_sb]):
        def f_cs1(wsb_=wsb_):
            def f(te):
                for k in range(16):
                    m = te.matmul(ptB[0][0:1, 0:256], onesb[:, :],
                                  wsb_[:, 256 * k:256 * (k + 1)],
                                  start=(k == 0), stop=(k == 15))
                return m
            return f
        w = [("qw", ("w_" + ["wq", "wk", "wv"][wi],)), ("vp", ("init",))]
        if wi > 0:
            w.append(("ap", ("cve", wi - 1)))
        A("T", f_cs1(), waits=w, inc=("tp", 1, ("cs1", wi)))

        def f_cs2(wi=wi):
            return lambda a: a.copy(colr[:, :], ptB[0][0:1, 0:256])
        A("A", f_cs2(), waits=[("tp", ("cs1", wi))], inc=("ap", 1, ("cs2", wi)))

        def f_cs3(wi=wi):
            return lambda te: te.matmul(ptB[0][:, 256:512], ones1b[:, :], colr[:, :],
                                        start=True, stop=True)
        A("T", f_cs3(), waits=[("ap", ("cs2", wi))], inc=("tp", 1, ("cs3", wi)))

        def f_cve(wi=wi):
            return lambda a: a.copy(cvb[wi][:, :], ptB[0][:, 256:512])
        A("A", f_cve(), waits=[("tp", ("cs3", wi))],
          inc=("ap", 1, ("cve", wi)))
    A("T", lambda te: te.nop(), waits=[("ap", ("cve", 2))], inc=("tp", 1, ("cvb",)))

    for t in range(NT + 1):
        tb = t % 2
        u = t - 1
        ub = u % 2

        if t < NT:
            def f_dx(t=t, tb=tb):
                return lambda e: dma(e, xf[tb][:, :], x_full[t * 128:(t + 1) * 128, :])
            w = []
            if t >= 2:
                w = [("vp", ("v1", t - 2)), ("ap", ("sq", t - 2))]
            A("S", f_dx(), waits=w, inc=("qx", 16, ("x", t)))

            def f_dxt(t=t, tb=tb):
                def f(e):
                    return dma(e, xtb[tb][:, :].rearrange("p (c tk) -> p c tk", c=16),
                               x_t[:, t * 128:(t + 1) * 128]
                               .rearrange("(c p) tk -> p c tk", p=128))
                return f
            w = [("tp", ("qkv", t - 2))] if t >= 2 else []
            A("G", f_dxt(), waits=w, inc=("qw", 16, ("xt", t)))

            def f_dc(t=t, tb=tb):
                return lambda e: dma(e, c2s[tb][:, :], c2_t[t * 128:(t + 1) * 128, :])
            def f_ds(t=t, tb=tb):
                return lambda e: dma(e, s2s[tb][:, :], s2_t[t * 128:(t + 1) * 128, :])
            w = [("vp", ("ropek", t - 2))] if t >= 2 else []
            A("G", f_dc(), waits=w, inc=("qw", 16, ("tabc", t)))
            A("G", f_ds(), waits=[], inc=("qw", 16, ("tab", t)))

            def f_sq(t=t, tb=tb):
                return lambda e: e.activation(dump[:, :], xf[tb][:, :], AF.Square,
                                              accum_out=ssq[tb][:, :])
            A("A", f_sq(), waits=[("qx", ("x", t))], inc=("ap", 1, ("sq", t)))

            def f_qkv(t=t, tb=tb):
                def f(te):
                    for (ps, wsb) in [(Qp[t % 2], wq_sb), (Kp[t % 2], wk_sb),
                                      (Vp[t % 2], wv_sb)]:
                        for k in range(16):
                            m = te.matmul(ps[:, :], xtb[tb][:, 128 * k:128 * (k + 1)],
                                          wsb[:, 256 * k:256 * (k + 1)],
                                          start=(k == 0), stop=(k == 15))
                    return m
                return f
            w = [("qw", ("xt", t)), ("qw", ("w_wq",)), ("qw", ("w_wk",)),
                 ("qw", ("w_wv",)), ("tp", ("cvb",))]
            if t >= 2:
                w.append(("ap", ("scl", t - 2)))
            A("T", f_qkv(), waits=w, inc=("tp", 1, ("qkv", t)))

        if u >= 0:
            def f_nrm(u=u, ub=ub):
                def f(v):
                    FV(v, v.tensor_scalar(nrmb[ub][:, :], meanb[ub][:, :],
                                          rstd[ub][:, :], -1.0,
                                          ALU.mult, ALU.mult))
                    return v.engine_nop()
                return f
            A("V", f_nrm(), waits=[("ap", ("rstd", u))], inc=("vp", 1, ("nrm", u)))

        if t < NT:
            def f_red(t=t, tb=tb):
                def f(v):
                    r_ = v.reduce_sum(srow[tb][:, :], xf[tb][:, :], AX)
                    if t == 0:
                        FV(v, r_)
                        r_ = v.engine_nop()
                    return r_
                return f
            A("V", f_red(), waits=[("qx", ("x", t))], inc=("vp", 1, None))

        if u >= 0:
            def f_scl(ps, dst, u=u, ub=ub):
                return lambda a: a.mul(dst[:, :], ps[:, :], rstd[ub][:, :])
            wA = [("tp", ("qkv", u)), ("ap", ("rstd", u))]
            if u >= 1:
                wA.append(("vp", ("ropek", u - 1)))
            A("A", f_scl(Qp[u % 2], qsf), waits=wA, inc=("ap", 1, None))
            A("A", f_scl(Kp[u % 2], ksf), waits=[], inc=("ap", 1, None))
            A("A", f_scl(Vp[u % 2], vsf), waits=[], inc=("ap", 1, ("scl", u)))

            def f_rope(sf, cv, dst, u=u, ub=ub):
                def f(v):
                    v.scalar_tensor_tensor(rt0[:, :], cv[:, :], nrmb[ub][:, :],
                                           sf[:, :], ALU.mult, ALU.add)
                    v.tensor_tensor(rt1[:, :], rt0[:, :], c2s[ub][:, :], ALU.mult)
                    pin = rt0[:, :].rearrange("p (h s i) -> p h s i", h=2, s=2)
                    sin_ = s2s[ub][:, :].rearrange("p (h s i) -> p h s i", h=2, s=2)
                    rto = rt2[:, :].rearrange("p (h s i) -> p h s i", h=2, s=2)
                    v.tensor_tensor(rto[:, :, 0, :], pin[:, :, 1, :],
                                    sin_[:, :, 0, :], ALU.mult)
                    FV(v, v.tensor_tensor(rto[:, :, 1, :], pin[:, :, 0, :],
                                          sin_[:, :, 1, :], ALU.mult))
                    return v.tensor_tensor(dst[:, :], rt1[:, :], rt2[:, :], ALU.add)
                return f
            wq_ = [("ap", ("scl", u)), ("qw", ("tab", u)), ("qw", ("tabc", u))]
            if u >= 2:
                wq_ += [("tp", ("trqk", u - 2)), ("ap", ("nsq", u - 2))]
            A("V", f_rope(qsf, cvb[0], qr[u % 2]), waits=wq_,
              inc=("vp", 1, ("ropeq", u)))

        if t < NT:
            def f_mean(t=t, tb=tb):
                def f(v):
                    FV(v, v.tensor_scalar(meanb[tb][:, :], srow[tb][:, :],
                                          1.0 / D, None, ALU.mult))
                    return v.engine_nop()
                return f
            A("V", f_mean(), waits=[], inc=("vp", 1, None))

        if u >= 0:
            A("V", f_rope(ksf, cvb[1], kr[u % 2]), waits=[],
              inc=("vp", 1, ("ropek", u)))

            def f_nsq(u=u):
                def f(a):
                    for src_, col in ((qr[u % 2], 0), (kr[u % 2], 1)):
                        for h in range(2):
                            m = a.activation(
                                dump[:, 128 * (2 * col + h):128 * (2 * col + h + 1)],
                                src_[:, 128 * h:128 * (h + 1)], AF.Square,
                                accum_out=nsqb[:, 2 * col + h:2 * col + h + 1])
                    return m
                return f
            A("A", f_nsq(), waits=[("vp", ("ropek", u))], inc=("ap", 1, ("nsq", u)))

        if t < NT:
            def f_m2(t=t, tb=tb):
                return lambda v: v.tensor_scalar(m2b[:, :], meanb[tb][:, :],
                                                 meanb[tb][:, :], None, ALU.mult)
            A("V", f_m2(), waits=[], inc=("vp", 1, None))

        if u >= 0:
            def f_vev(u=u, ub=ub):
                def f(v):
                    return v.scalar_tensor_tensor(Vb[:, u * 256:(u + 1) * 256],
                                                  cvb[2][:, :], nrmb[ub][:, :],
                                                  vsf[:, :], ALU.mult, ALU.add)
                return f
            A("V", f_vev(), waits=[("ap", ("scl", u))], inc=("vp", 1, ("vev", u)))

            if u + 1 < NT:
                def f_ve1(t=u + 1, tb=(u + 1) % 2):
                    return lambda v: v.tensor_scalar(vepsb[:, :], ssq[tb][:, :],
                                                     1.0 / D, None, ALU.mult)
                A("V", f_ve1(), waits=[("ap", ("sq", u + 1))], inc=("vp", 1, None))

            def f_nmax(u=u):
                def f(v):
                    v.tensor_tensor(nsqb[:, 4:5], nsqb[:, 0:1], nsqb[:, 1:2], ALU.max)
                    v.tensor_tensor(nsqb[:, 5:6], nsqb[:, 2:3], nsqb[:, 3:4], ALU.max)
                    v.tensor_tensor(mqr[:, 0:1], mqr[:, 0:1], nsqb[:, 4:5], ALU.max)
                    return v.tensor_tensor(mqr[:, 1:2], mqr[:, 1:2], nsqb[:, 5:6],
                                           ALU.max)
                return f
            A("V", f_nmax(), waits=[("ap", ("nsq", u)), ("vp", ("init",))],
              inc=("vp", 1, ("nmax", u)))

        if t == 0:
            def f_ve1z(v):
                r_ = v.tensor_scalar(vepsb[:, :], ssq[0][:, :], 1.0 / D,
                                     None, ALU.mult)
                FV(v, r_)
                return v.engine_nop()
            A("V", f_ve1z, waits=[("ap", ("sq", 0))], inc=("vp", 1, None))

        if u >= 0:
            def f_trqk(u=u):
                def f(te):
                    for h in range(2):
                        te.matmul(ptB[u % 2][:, 128 * h:128 * (h + 1)],
                                  qr[u % 2][:, 128 * h:128 * (h + 1)], identb[:, :],
                                  start=True, stop=True)
                    for h in range(2):
                        m = te.matmul(ptB[u % 2][:, 256 + 128 * h:256 + 128 * (h + 1)],
                                      kr[u % 2][:, 128 * h:128 * (h + 1)], identb[:, :],
                                      start=True, stop=True)
                    return m
                return f
            w = [("vp", ("ropeq", u)), ("vp", ("ropek", u)), ("qw", ("w_id",))]
            if u >= 2:
                w.append(("ap", ("qktev", u - 2)))
            A("T", f_trqk(), waits=w, inc=("tp", 1, ("trqk", u)))

        if t < NT:
            def f_ve2(t=t, tb=tb):
                return lambda v: v.scalar_tensor_tensor(vepsb[:, :], m2b[:, :], -1.0,
                                                        vepsb[:, :], ALU.mult, ALU.add)
            A("V", f_ve2(), waits=[], inc=("vp", 1, ("v1", t)))

            def f_rstd(t=t, tb=tb):
                def f(e):
                    FA(e, e.activation(lnvb[tb][:, :], vepsb[:, :], AF.Ln,
                                       bias=epsb[:, :]))
                    return e.activation(rstd[tb][:, :], lnvb[tb][:, :], AF.Exp,
                                        scale=-0.5)
                return f
            A("A", f_rstd(), waits=[("vp", ("v1", t))], inc=("ap", 1, ("rstd", t)))

        if u >= 0:
            def f_qktev(u=u):
                def f(a):
                    qdst = QT[:, :].rearrange("p (h tk) -> p h tk", h=2)[:, :, u * 128:(u + 1) * 128]
                    kdst = KT[:, :].rearrange("p (h tk) -> p h tk", h=2)[:, :, u * 128:(u + 1) * 128]
                    qsrc = ptB[u % 2][:, 0:256].rearrange("p (h x) -> p h x", h=2)
                    ksrc = ptB[u % 2][:, 256:512].rearrange("p (h x) -> p h x", h=2)
                    a.copy(qdst, qsrc)
                    return a.copy(kdst, ksrc)
                return f
            A("A", f_qktev(), waits=[("tp", ("trqk", u))], inc=("ap", 1, ("qktev", u)))

    A("V", lambda v: v.tensor_copy(mqrb[:, :], mqr[:, :]),
      waits=[("vp", ("nmax", NT - 1))], inc=("vp", 1, ("mqb",)))
    def f_ntr(te):
        te.matmul(ptB[0][0:1, 0:128], mqrb[:, 0:1], identb[:, :],
                  start=True, stop=True)
        return te.matmul(ptB[0][0:1, 128:256], mqrb[:, 1:2], identb[:, :],
                         start=True, stop=True)
    A("T", f_ntr, waits=[("vp", ("mqb",)), ("vp", ("qktev", NT - 2))],
      inc=("tp", 1, ("ntr",)))

    def f_nred(v):
        FV(v, v.reduce_max(mq2[0:1, 0:1], ptB[0][0:1, 0:128], AX))
        FV(v, v.reduce_max(mq2[0:1, 1:2], ptB[0][0:1, 128:256], AX))
        return v.tensor_tensor(sc1[:, 1:2], mq2[:, 0:1], mq2[:, 1:2], ALU.mult)
    A("V", f_nred, waits=[("tp", ("ntr",))], inc=("vp", 1, ("b1",)))

    def f_b2(e):
        FA(e, e.activation(sc1[:, 2:3], sc1[:, 1:2], AF.Ln))
        return e.activation(sc1[:, 4:5], sc1[:, 2:3], AF.Exp, scale=0.5)
    A("A", f_b2, waits=[("vp", ("b1",))], inc=("ap", 1, ("b2",)))

    def f_b3(v):
        FV(v, v.tensor_scalar(sc1[:, 3:4], sc1[:, 4:5], -1.002 * SCL, -1.0 * SCL,
                              ALU.mult, ALU.add))
        return v.tensor_copy(sc1b[:, :], sc1[:, 3:4])
    A("V", f_b3, waits=[("ap", ("b2",))], inc=("vp", 1, ("b3",)))

    A("T", lambda te: te.matmul(ptB[0][:, 0:1], ones1b[:, :], sc1b[:, :],
                                start=True, stop=True),
      waits=[("vp", ("b3",))], inc=("tp", 1, ("b4",)))

    A("V", lambda v: v.tensor_copy(bias_sb[:, :], ptB[0][:, 0:1]),
      waits=[("tp", ("b4",))], inc=("vp", 1, ("bias",)))

    p1.close()

    p2a = ExitStack()
    attnT = sbt(p2a, "attnT", [128, 2 * S], BF16)
    PTs = sbt(p2a, "PTs", [128, 2 * 2048], BF16)
    otsb = sbt(p2a, "otsb", [128, 512], BF16)
    arow = sbt(p2a, "arow", [128, 512], BF16)
    rrow = sbt(p2a, "rrow", [1, 512], F32)
    rrow2 = sbt(p2a, "rrow2", [1, 512], F32)
    rtl = sbt(p2a, "rtl", [128, 4], F32)
    stp = pst(p2a, "stp", [128, 2048])
    outp = [pst(p2a, f"outp{i}", [128, 512]) for i in range(2)]
    sump = pst(p2a, "sump", [1, 512])
    ptr1 = pst(p2a, "ptr1", [128, 512])

    quads = []
    groups = []
    for g in range(NB):
        for h in range(2):
            groups.append((g, h))
            for qd in range(g + 1):
                quads.append((g, h, qd))
    qdidx = {x: i for i, x in enumerate(quads)}
    gidx = {gh: i for i, gh in enumerate(groups)}

    pending_oav = []
    pending_dance = [None]

    def emit_dance(gh):
        g, h = gh
        gi = gidx[gh]

        def f_rtr(g=g, h=h, gi=gi):
            def f(te):
                for j in range(4):
                    te.matmul(ptr1[:, 128 * j:128 * (j + 1)],
                              otsb[:, 128 * j:128 * (j + 1)], identb[:, :],
                              start=True, stop=True)
                for j in range(4):
                    m = te.matmul(outp[gi % 2][:, 504 + j:505 + j],
                                  rrow2[:, 128 * j:128 * (j + 1)], onesf1[:, :],
                                  start=True, stop=True)
                return m
            return f
        A("T", f_rtr(), waits=[("vp", ("sev", g, h))], inc=("tp", 1, ("rtr", g, h)))

        def f_rsc(g=g, h=h, gi=gi):
            def f(v):
                FV(v, v.tensor_copy(rtl[:, :], outp[gi % 2][:, 504:508]))
                for j in range(4):
                    m = v.tensor_scalar(arow[:, 128 * j:128 * (j + 1)],
                                        ptr1[:, 128 * j:128 * (j + 1)],
                                        rtl[:, j:j + 1], None, ALU.mult)
                return m
            return f
        A("V", f_rsc(), waits=[("tp", ("rtr", g, h))], inc=("vp", 1, ("rsc", g, h)))

        def f_tb(g=g, h=h):
            def f(te):
                for j in range(4):
                    m = te.matmul(ptr1[:, 128 * j:128 * (j + 1)],
                                  arow[:, 128 * j:128 * (j + 1)], identb[:, :],
                                  start=True, stop=True)
                return m
            return f
        A("T", f_tb(), waits=[("vp", ("rsc", g, h))], inc=("tp", 1, ("tb", g, h)))

        def f_fin(g=g, h=h):
            return lambda v: v.tensor_copy(
                attnT[:, h * S + g * 512: h * S + (g + 1) * 512], ptr1[:, :])
        A("V", f_fin(), waits=[("tp", ("tb", g, h))], inc=("vp", 1, ("fin", g, h)))

    def flush_oav():
        fo, wo_, io_ = pending_oav.pop(0)
        A("T", fo, waits=wo_, inc=io_)
        if pending_dance[0] is not None:
            emit_dance(pending_dance[0])
            pending_dance[0] = None

    for g in range(NB):
        nq = g + 1
        for h in range(2):
            gi = gidx[(g, h)]
            for qd in range(nq):
                Q = qdidx[(g, h, qd)]
                masked = qd == nq - 1

                def f_st(g=g, h=h, qd=qd, Q=Q):
                    def f(te):
                        for s_ in range(4):
                            c = 4 * qd + s_
                            m = te.matmul(
                                stp[:, 512 * s_:512 * (s_ + 1)],
                                KT[:, h * S + c * 128: h * S + (c + 1) * 128],
                                QT[:, h * S + g * 512: h * S + (g + 1) * 512],
                                start=True, stop=True)
                        return m
                    return f
                w = []
                if Q >= 1:
                    w.append(("ap", ("exp", Q - 1)))
                if Q == 0:
                    w.append(("vp", ("qktev", NT - 1)))
                A("T", f_st(), waits=w, inc=("tp", 1, ("st", Q)))

                def f_exp(Q=Q):
                    return lambda e: e.activation(
                        PTs[:, (Q % 2) * 2048:(Q % 2 + 1) * 2048],
                        stp[:, :], AF.Exp, bias=bias_sb[:, :], scale=SCL)
                w = [("tp", ("st", Q)), ("vp", ("bias",))]
                if Q >= 2:
                    w.append(("tp", ("av", Q - 2)))
                A("A", f_exp(), waits=w, inc=("ap", 1, ("exp", Q)))

                if masked:
                    def f_msk(Q=Q):
                        def f(v):
                            sl = PTs[:, (Q % 2) * 2048:(Q % 2 + 1) * 2048]
                            return v.tensor_tensor(sl, sl, masks[:, :], ALU.mult)
                        return f
                    A("V", f_msk(), waits=[("ap", ("exp", Q)), ("qw", ("w_const",))],
                      inc=("vp", 1, ("msk", Q)))

                def f_oav(g=g, h=h, qd=qd, Q=Q, gi=gi, nq=nq):
                    def f(te):
                        for s_ in range(4):
                            c = 4 * qd + s_
                            pt = PTs[:, (Q % 2) * 2048 + 512 * s_:
                                     (Q % 2) * 2048 + 512 * (s_ + 1)]
                            te.matmul(sump[:, :], onesb[:, :], pt,
                                      start=(qd == 0 and s_ == 0),
                                      stop=(qd == nq - 1 and s_ == 3),
                                      skip_group_check=True)
                            m = te.matmul(
                                outp[gi % 2][:, :],
                                Vb[:, c * 256 + h * 128: c * 256 + (h + 1) * 128],
                                pt,
                                start=(qd == 0 and s_ == 0),
                                stop=(qd == nq - 1 and s_ == 3),
                                skip_group_check=True)
                        return m
                    return f
                w = [("vp", ("msk", Q))] if masked else [("ap", ("exp", Q))]
                if qd == 0:
                    if gi >= 1:
                        w.append(("vp", ("sev1", groups[gi - 1])))
                    if gi >= 2:
                        w.append(("vp", ("rsc", *groups[gi - 2])))
                pending_oav.append((f_oav(), w, ("tp", 1, ("av", Q))))
                if len(pending_oav) > 1:
                    flush_oav()

            while pending_oav:
                flush_oav()

            ql = qdidx[(g, h, nq - 1)]

            def f_seva(g=g, h=h):
                def f(v):
                    FV(v, v.tensor_copy(rrow[:, :], sump[:, :]))
                    return v.reciprocal(rrow2[:, :], rrow[:, :])
                return f
            A("V", f_seva(), waits=[("tp", ("av", ql))],
              inc=("vp", 1, ("sev1", (g, h))))

            def f_sevb(g=g, h=h, gi=gi):
                return lambda v: v.tensor_copy(otsb[:, :], outp[gi % 2][:, :])
            A("V", f_sevb(), waits=[], inc=("vp", 1, ("sev", g, h)))

            pending_dance[0] = (g, h)

    emit_dance(pending_dance[0])
    pending_dance[0] = None

    if _DEBUG:
        A("S", lambda e: dma(e, dbg_qt[:, :], QT[:, :]),
          waits=[("vp", ("qktev", NT - 1))], inc=("sto", 16, None))
        A("S", lambda e: dma(e, dbg_kt[:, :], KT[:, :]),
          waits=[], inc=("sto", 16, None))
        A("S", lambda e: dma(e, dbg_vb[:, :], Vb[:, :]),
          waits=[], inc=("sto", 16, None))
        A("S", lambda e: e.dma_start(out=dbg_bias[:, 0:1], in_=bias_sb[:, :],
                                     allow_slow_non_contiguous=True),
          waits=[("vp", ("bias",))], inc=("sto", 16, None))
        A("S", lambda e: dma(e, dbg_bias[0:1, 1:3], mq2[:, :]),
          waits=[], inc=("sto", 16, None))
        A("S", lambda e: dma(e, dbg_at[:, :], attnT[:, :]),
          waits=[("vp", ("fin", NB - 1, 1))], inc=("sto", 16, None))
    for j in range(NCORE):
        def f_stg(j=j):
            def f(e):
                src = attnT[:, :].rearrange("p (h tk) -> p h tk", h=2)[:, :, j * 1024:(j + 1) * 1024]
                dst = a2a_in[j * 256:(j + 1) * 256, :].rearrange("(h p) tk -> p h tk", p=128)
                return dma(e, dst, src)
            return f
        A("S", f_stg(), waits=[("vp", ("fin", NB - 1, 1))], inc=("qx", 16, ("stg", j)))

    for m_pre in range(2):
        for a_pre, srcw in ((0, eg_w), (1, eu_w)):
            def f_pre(m_pre=m_pre, a_pre=a_pre, srcw=srcw):
                def f(g):
                    return dma(g, guW[a_pre][m_pre][:, :]
                               .rearrange("p (c i) -> p c i", c=16),
                               srcw[0][:, m_pre * 128:(m_pre + 1) * 128]
                               .rearrange("(c p) i -> p c i", p=128))
                return f
            key = (("guw", ("e", 0), m_pre) if a_pre == 1
                   else ("guwg", ("e", 0), m_pre))
            A("G", f_pre(), waits=[], inc=("qw", 16, key))

    def f_cc(g):
        with allow_ici_all_to_all():
            return g.collective_compute(
                "AllToAll", ALU.bypass, replica_groups=[list(range(NCORE))],
                ins=[a2a_in.ap().opt()], outs=[a2a_out.ap().opt()])
    A("G", f_cc, waits=[("qx", ("stg", j_)) for j_ in range(NCORE)],
      inc=("cc", 1, ("cc",)))

    if _DEBUG:
        A("S", lambda e: dma(e, dbg_a2a[:, :], a2a_out.ap()),
          waits=[("cc", 1)], inc=("sto", 16, None))
    p2a.close()
    qkv_sc.close()

    p2b = ExitStack()
    acc = sbt(p2b, "acc", [128, NTO * D], F32)
    comb = sbt(p2b, "comb", [128, NTO * 8], F32)
    gwsb = sbt(p2b, "gwsb", [128, 16 * 8], F32)
    lgsb = sbt(p2b, "lgsb", [128, 8], F32)
    cm1 = sbt(p2b, "cm1", [128, 1], F32)
    cm1n = sbt(p2b, "cm1n", [128, 1], F32)
    cm2 = sbt(p2b, "cm2", [128, 1], F32)
    ceq = sbt(p2b, "ceq", [128, 8], F32)
    ce1 = sbt(p2b, "ce1", [128, 8], F32)
    cge = sbt(p2b, "cge", [128, 8], F32)
    cden = sbt(p2b, "cden", [128, 1], F32)
    cdenr = sbt(p2b, "cdenr", [128, 1], F32)
    tmp8 = sbt(p2b, "tmp8", [128, 8], F32)
    gu_sc = ExitStack()
    xnTo = sbt(gu_sc, "xnTo", [128, 16 * TOK], BF16)
    tsil = [sbt(gu_sc, f"tsil{i}", [128, 512], F32) for i in range(2)]

    ln2 = ExitStack()
    xnf = [sbt(ln2, f"xnf{i}", [128, D], F32) for i in range(2)]
    xnTf = sbt(ln2, "xnTf", [128, 16 * 128], F32)
    ptA2 = [pst(ln2, f"ptA2{i}", [128, 512]) for i in range(2)]
    lgps = pst(ln2, "lgps", [128, 8])
    ptF = pst(ln2, "ptF", [128, 512])

    A("S", lambda e: dma(e, gwsb[:, :].rearrange("p (c e) -> p c e", c=16),
                         gate_wTd.ap().rearrange("(c p) e -> p c e", p=128)),
      waits=[("qx", ("stg", j_)) for j_ in range(NCORE)], inc=("qx", 16, ("gw",)))
    A("S", lambda e: dma(e, acc[:, :].rearrange("p (t f) -> p t f", t=NTO),
                         x_own.ap().rearrange("(t p) f -> p t f", p=128)),
      waits=[("qx", ("stg", j_)) for j_ in range(NCORE)],
      inc=("qx", 16, ("acc0",)))

    for t in range(NTO):
        tb = t % 2
        gt = 1024 + t

        def f_dx2(t=t, tb=tb):
            return lambda e: dma(e, xf[tb][:, :], x_own[t * 128:(t + 1) * 128, :])
        if t >= 2:
            w = [("vp", ("apply", gt - 2)), ("ap", ("sq", gt - 2))]
        else:
            w = [("vp", ("v1", 62 + t)), ("ap", ("sq", 62 + t))]
        A("G", f_dx2(), waits=w, inc=("qw", 16, ("x", gt)))

        def f_sq2(t=t, tb=tb):
            return lambda e: e.activation(dump[:, :], xf[tb][:, :], AF.Square,
                                          accum_out=ssq[tb][:, :])
        A("A", f_sq2(), waits=[("qx", ("x", gt))], inc=("ap", 1, ("sq", gt)))

        def f_stats2(t=t, tb=tb):
            def f(v):
                FV(v, v.reduce_sum(srow[tb][:, :], xf[tb][:, :], AX))
                FV(v, v.tensor_scalar(meanb[tb][:, :], srow[tb][:, :], 1.0 / D,
                                      None, ALU.mult))
                FV(v, v.tensor_scalar(m2b[:, :], meanb[tb][:, :], meanb[tb][:, :],
                                      None, ALU.mult))
                FV(v, v.tensor_scalar(vepsb[:, :], ssq[tb][:, :], 1.0 / D, None,
                                      ALU.mult))
                FV(v, v.tensor_tensor(vepsb[:, :], vepsb[:, :], m2b[:, :],
                                      ALU.subtract))
                return v.tensor_scalar(vepsb[:, :], vepsb[:, :], EPS, None, ALU.add)
            return f
        A("V", f_stats2(), waits=[("qx", ("x", gt)), ("ap", ("sq", gt))],
          inc=("vp", 1, ("v1", gt)))

        def f_rstd2(t=t, tb=tb):
            def f(e):
                FA(e, e.activation(lnvb[tb][:, :], vepsb[:, :], AF.Ln))
                return e.activation(rstd[tb][:, :], lnvb[tb][:, :], AF.Exp, scale=-0.5)
            return f
        A("A", f_rstd2(), waits=[("vp", ("v1", gt))], inc=("ap", 1, ("rstd", gt)))

        def f_apply2(t=t, tb=tb):
            def f(v):
                v.tensor_scalar(xn[tb][:, :], xf[tb][:, :], meanb[tb][:, :],
                                rstd[tb][:, :], ALU.subtract, ALU.mult)
                return v.tensor_scalar(xnf[tb][:, :], xf[tb][:, :], meanb[tb][:, :],
                                       rstd[tb][:, :], ALU.subtract, ALU.mult)
            return f
        A("V", f_apply2(), waits=[("ap", ("rstd", gt))], inc=("vp", 1, ("apply", gt)))

        for j in range(4):
            gi = 1024 + t * 4 + j

            def f_tr2(t=t, tb=tb, j=j, gi=gi):
                def f(te):
                    for i in range(4):
                        c = 4 * j + i
                        m = te.matmul(ptA2[gi % 2][:, 128 * i:128 * (i + 1)],
                                      xn[tb][:, 128 * c:128 * (c + 1)],
                                      identb[:, :], start=True, stop=True)
                    return m
                return f
            w = [("vp", ("apply", gt))]
            if gi >= 1026:
                w.append(("vp", ("trev", gi - 2)))
            A("T", f_tr2(), waits=w, inc=("tp", 1, ("trb", gi)))

            def f_trev2(t=t, j=j, gi=gi):
                def f(v):
                    src = ptA2[gi % 2][:, :].rearrange("p (c i) -> p c i", c=4)
                    dst = xnTo[:, :].rearrange("p (c tk) -> p c tk", c=16)[
                        :, 4 * j:4 * j + 4, t * 128:(t + 1) * 128]
                    return v.tensor_copy(dst, src)
                return f
            A("V", f_trev2(), waits=[("tp", ("trb", gi))], inc=("vp", 1, ("trev", gi)))

        for j in range(4):
            def f_trfj(t=t, tb=tb, j=j):
                def f(te):
                    for i in range(4):
                        c = 4 * j + i
                        m = te.matmul(ptF[:, 128 * i:128 * (i + 1)],
                                      xnf[tb][:, 128 * c:128 * (c + 1)],
                                      identf[:, :], start=True, stop=True)
                    return m
                return f
            w = [("vp", ("apply", gt)), ("qw", ("w_idf",))]
            if j >= 1:
                w.append(("vp", ("trfe", gt, j - 1)))
            elif t >= 1:
                w.append(("tp", ("lg", t - 1)))
            A("T", f_trfj(), waits=w, inc=("tp", 1, ("trf", gt, j)))

            def f_trfe(t=t, j=j):
                def f(v):
                    return v.tensor_copy(xnTf[:, j * 512:(j + 1) * 512], ptF[:, :])
                return f
            w = [("tp", ("trf", gt, j))]
            if j == 0 and t >= 1:
                w.append(("tp", ("lg", t - 1)))
            A("V", f_trfe(), waits=w, inc=("vp", 1, ("trfe", gt, j)))

        def f_lg(t=t):
            def f(te):
                for k in range(16):
                    m = te.matmul(lgps[:, :], xnTf[:, 128 * k:128 * (k + 1)],
                                  gwsb[:, 8 * k:8 * (k + 1)],
                                  start=(k == 0), stop=(k == 15))
                return m
            return f
        w = [("vp", ("trfe", gt, 3)), ("qx", ("gw",))]
        if t >= 1:
            w.append(("vp", ("cmb1", t - 1)))
        A("T", f_lg(), waits=w, inc=("tp", 1, ("lg", t)))

        def f_comb1(t=t):
            def f(v):
                FV(v, v.tensor_copy(lgsb[:, :], lgps[:, :]))
                FV(v, v.reduce_max(cm1[:, :], lgsb[:, :], AX))
                v.tensor_scalar(cm1n[:, :], cm1[:, :], -1.0, None, ALU.mult)
                FV(v, v.tensor_scalar(ceq[:, :], lgsb[:, :], cm1[:, :], None,
                                      ALU.is_ge))
                FV(v, v.tensor_scalar(ceq[:, :], ceq[:, :], -1e30, None, ALU.mult))
                FV(v, v.tensor_tensor(ceq[:, :], ceq[:, :], lgsb[:, :], ALU.add))
                return v.reduce_max(cm2[:, :], ceq[:, :], AX)
            return f
        w = [("tp", ("lg", t))]
        if t >= 1:
            w.append(("ap", ("cmb2", t - 1)))
        A("V", f_comb1(), waits=w, inc=("vp", 1, ("cmb1", t)))

        def f_comb2(t=t):
            def f(e):
                e.activation(ce1[:, :], lgsb[:, :], AF.Exp, bias=cm1n[:, :])
                return e.activation(cden[:, :], cm2[:, :], AF.Exp, bias=cm1n[:, :])
            return f
        A("A", f_comb2(), waits=[("vp", ("cmb1", t))], inc=("ap", 1, ("cmb2", t)))

        def f_comb3(t=t):
            def f(v):
                FV(v, v.tensor_scalar(cden[:, :], cden[:, :], 1.0, None, ALU.add))
                FV(v, v.reciprocal(cdenr[:, :], cden[:, :]))
                FV(v, v.tensor_scalar(cge[:, :], lgsb[:, :], cm2[:, :], None,
                                      ALU.is_ge))
                FV(v, v.tensor_scalar(tmp8[:, :], ce1[:, :], cdenr[:, :], None,
                                      ALU.mult))
                return v.tensor_tensor(comb[:, t * 8:(t + 1) * 8], tmp8[:, :],
                                       cge[:, :], ALU.mult)
            return f
        A("V", f_comb3(), waits=[("ap", ("cmb2", t))], inc=("vp", 1, ("cmb3", t)))

    if _DEBUG:
        A("S", lambda e: dma(e, dbg_cmb[:, :], comb[:, :]),
          waits=[("vp", ("cmb3", NTO - 1))], inc=("sto", 16, None))
        A("S", lambda e: dma(e, dbg_xnto[:, :], xnTo[:, :]),
          waits=[("vp", ("trev", 1024 + (NTO - 1) * 4 + 3))], inc=("sto", 16, None))
    ln2.close()
    GU = [pst(p2b, f"GU{i}", [128, 512]) for i in range(4)]
    EPm = [pst(p2b, f"EPm{i}", [128, 512]) for i in range(4)]

    exp_sc = ExitStack()
    actT = sbt(exp_sc, "actT", [128, 11 * TOK], BF16)
    dwn = [sbt(exp_sc, f"dwn{i}", [128, 11 * 512], BF16, side="right") for i in range(2)]

    gu_hist = []
    sil_hist = []
    ep_hist = []

    def emit_gu(tag, nm, src_g, src_u, actdst, e_idx, first_extra_wait,
                preloaded=0):
        for m in range(nm):
            li = len(gu_hist)
            gu_hist.append((tag, m))
            mb = li % 3
            skip_load = m < preloaded

            def f_wg(m=m, mb=mb):
                def f(g):
                    sl = src_g.ap() if e_idx is None else src_g[e_idx]
                    return dma(g, guW[0][mb][:, :].rearrange("p (c i) -> p c i", c=16),
                               sl[:, m * 128:(m + 1) * 128]
                               .rearrange("(c p) i -> p c i", p=128))
                return f

            def f_wu(m=m, mb=mb):
                def f(g):
                    sl = src_u.ap() if e_idx is None else src_u[e_idx]
                    return dma(g, guW[1][mb][:, :].rearrange("p (c i) -> p c i", c=16),
                               sl[:, m * 128:(m + 1) * 128]
                               .rearrange("(c p) i -> p c i", p=128))
                return f
            if not skip_load:
                w = []
                if li >= 3:
                    pt_, pm_ = gu_hist[li - 3]
                    w.append(("vp", ("gact", pt_, pm_, 1)))
                A("G", f_wg(), waits=w, inc=("qw", 16, ("guwg", tag, m)))
                A("G", f_wu(), waits=[], inc=("qw", 16, ("guw", tag, m)))

            for hh in range(2):
                def f_gu(m=m, mb=mb, hh=hh):
                    def f(te):
                        for k in range(16):
                            te.matmul(GU[hh][:, :],
                                      guW[0][mb][:, 128 * k:128 * (k + 1)],
                                      xnTo[:, k * TOK + hh * 512: k * TOK + hh * 512 + 512],
                                      start=(k == 0), stop=(k == 15))
                        for k in range(16):
                            m_ = te.matmul(GU[2 + hh][:, :],
                                           guW[1][mb][:, 128 * k:128 * (k + 1)],
                                           xnTo[:, k * TOK + hh * 512: k * TOK + hh * 512 + 512],
                                           start=(k == 0), stop=(k == 15))
                        return m_
                    return f
                w = [("qw", ("guw", tag, m)), ("qw", ("guwg", tag, m))]
                if m == 0 and hh == 0 and first_extra_wait:
                    w += first_extra_wait
                if m >= 1:
                    w.append(("vp", ("gact", tag, m - 1, hh)))
                A("T", f_gu(), waits=w, inc=("tp", 1, ("gu", tag, m, hh)))

                sli = len(sil_hist)
                sil_hist.append((tag, m, hh))

                def f_sil(m=m, hh=hh, sli=sli):
                    return lambda e: e.activation(tsil[sli % 2][:, :], GU[hh][:, :],
                                                  AF.Silu)
                w = [("tp", ("gu", tag, m, hh))]
                if sli >= 2:
                    w.append(("vp", ("gact", *sil_hist[sli - 2])))
                A("A", f_sil(), waits=w, inc=("ap", 1, ("sil", tag, m, hh)))

                def f_gact(m=m, hh=hh, sli=sli, actdst=actdst):
                    def f(v):
                        return v.tensor_tensor(
                            actdst[:, m * TOK + hh * 512: m * TOK + hh * 512 + 512],
                            tsil[sli % 2][:, :], GU[2 + hh][:, :], ALU.mult)
                    return f
                A("V", f_gact(), waits=[("ap", ("sil", tag, m, hh))],
                  inc=("vp", 1, ("gact", tag, m, hh)))

    def ep_prev_mark(kind):
        if not ep_hist:
            return None
        u = ep_hist[-1]
        if u[0] == "dn":
            return ("dev", u[1], u[2], u[3], 3)
        return ("swev", u[1], u[2], 3)

    def prefetch_guw(tag, src_g, src_u, e_idx):
        for m in range(2):
            li_pred = len(gu_hist) + m
            mb = li_pred % 3
            for a_, srcw in ((0, src_g), (1, src_u)):
                def f_pre(m=m, mb=mb, a_=a_, srcw=srcw, e_idx=e_idx):
                    def f(g):
                        sl = srcw.ap() if e_idx is None else srcw[e_idx]
                        return dma(g, guW[a_][mb][:, :]
                                   .rearrange("p (c i) -> p c i", c=16),
                                   sl[:, m * 128:(m + 1) * 128]
                                   .rearrange("(c p) i -> p c i", p=128))
                    return f
                w = [("vp", ("gact", *gu_hist[li_pred - 3], 1))]
                key = ("guw", tag, m) if a_ == 1 else ("guwg", tag, m)
                A("G", f_pre(), waits=w, inc=("qw", 16, key))

    dwn_hist = []
    for e in range(8):
        emit_gu(("e", e), 11, eg_w, eu_w, actT, e,
                [("vp", ("cmb3", NTO - 1)), ("vp", ("trev", 1024 + (NTO - 1) * 4 + 3))]
                if e == 0 else None, preloaded=2)
        if _DEBUG and e == 0:
            A("S", lambda e_: dma(e_, dbg_act[:, :], actT[:, :]),
              waits=[("vp", ("gact", ("e", 0), 10, 1))], inc=("sto", 16, None))
        for n in range(4):
            dli = len(dwn_hist)
            dwn_hist.append((e, n))
            db = dli % 2

            def f_dw(e=e, n=n, db=db):
                def f(g):
                    return dma(g, dwn[db][:, :].rearrange("p (c f) -> p c f", c=11),
                               ed_w[e][:, n * 512:(n + 1) * 512]
                               .rearrange("(c p) f -> p c f", p=128))
                return f
            w = []
            if dli == 0:
                w.append(("tp", ("lg", NTO - 1)))
            if dli >= 2:
                pe_, pn_ = dwn_hist[dli - 2]
                w.append(("tp", ("dn", pe_, pn_, 1)))
            A("G", f_dw(), waits=w, inc=("qw", 16, ("dw", e, n)))

            for half in range(2):
                pm = ep_prev_mark("dn")
                ep_hist.append(("dn", e, n, half))

                def f_down(e=e, n=n, db=db, half=half):
                    def f(te):
                        for c in range(11):
                            for ti in range(4):
                                tile = half * 4 + ti
                                m = te.matmul(
                                    EPm[ti][:, :],
                                    actT[:, c * TOK + tile * 128: c * TOK + (tile + 1) * 128],
                                    dwn[db][:, c * 512:(c + 1) * 512],
                                    start=(c == 0), stop=(c == 10),
                                    skip_group_check=True)
                        return m
                    return f
                w = [("qw", ("dw", e, n)), ("vp", ("gact", ("e", e), 10, 1))]
                if pm is not None:
                    w.append(("vp", pm))
                A("T", f_down(), waits=w, inc=("tp", 1, ("dn", e, n, half)))

                for ti in range(4):
                    def f_dev(e=e, n=n, half=half, ti=ti):
                        def f(v):
                            tile = half * 4 + ti
                            sl = acc[:, tile * D + n * 512: tile * D + n * 512 + 512]
                            return v.scalar_tensor_tensor(
                                sl, EPm[ti][:, :],
                                comb[:, tile * 8 + e: tile * 8 + e + 1],
                                sl, ALU.mult, ALU.add)
                        return f
                    w = [("tp", ("dn", e, n, half)), ("qx", ("acc0",))]
                    A("V", f_dev(), waits=w, inc=("vp", 1, ("dev", e, n, half, ti)))

            if n == 1:
                if e < 7:
                    prefetch_guw(("e", e + 1), eg_w, eu_w, e + 1)
                else:
                    prefetch_guw(("s", 0), sg_w, su_w, None)

    exp_sc.close()
    sh_sc = ExitStack()
    actsh = sbt(sh_sc, "actsh", [128, 22 * TOK], BF16, side="right")

    emit_gu(("s", 0), 22, sg_w, su_w, actsh, None, None, preloaded=2)

    gu_sc.close()
    sw_sc = ExitStack()
    sdw = sbt(sw_sc, "sdw", [128, 22 * 512], BF16)
    wow = sbt(sw_sc, "wow", [128, 16 * 512], BF16)
    maw = [sbt(sw_sc, f"maw{i}", [128, TOK], BF16) for i in range(3)]

    maw_hist = []
    for n in range(4):
        def f_sdw(n=n):
            def f(g):
                return dma(g, sdw[:, :].rearrange("p (c f) -> p c f", c=22),
                           sd_w[:, n * 512:(n + 1) * 512]
                           .rearrange("(c p) f -> p c f", p=128))
            return f
        w = ([("tp", ("sd", n - 1, 1))] if n >= 1
             else [("tp", ("dn", 7, 3, 1)), ("tp", ("gu", ("s", 0), 21, 1))])
        A("G", f_sdw(), waits=w, inc=("qw", 16, ("sdw", n)))

        def f_wow(n=n):
            def f(g):
                return dma(g, wow[:, :].rearrange("p (c f) -> p c f", c=16),
                           wo_full[:, n * 512:(n + 1) * 512]
                           .rearrange("(c p) f -> p c f", p=128))
            return f
        A("G", f_wow(), waits=[], inc=("qw", 16, ("wow", n)))

        for half in range(2):
            pm = ep_prev_mark("sw")
            ep_hist.append(("sw", n, half))

            def f_sd(n=n, half=half):
                def f(te):
                    for c in range(22):
                        for ti in range(4):
                            tile = half * 4 + ti
                            m = te.matmul(
                                EPm[ti][:, :],
                                actsh[:, c * TOK + tile * 128: c * TOK + (tile + 1) * 128],
                                sdw[:, c * 512:(c + 1) * 512],
                                start=(c == 0), stop=False,
                                skip_group_check=True)
                    return m
                return f
            w = [("qw", ("sdw", n)), ("vp", ("gact", ("s", 0), 21, 1))]
            if pm is not None:
                w.append(("vp", pm))
            A("T", f_sd(), waits=w, inc=("tp", 1, ("sd", n, half)))

            for jj in range(16):
                mli = len(maw_hist)
                maw_hist.append((n, half, jj))

                def f_maw(n=n, half=half, jj=jj, mli=mli):
                    return lambda e: dma(e, maw[mli % 3][:, :],
                                         a2a_out[jj * 128:(jj + 1) * 128, :])
                w = [("cc", 1)]
                if mli < 3:
                    w += [("tp", ("dn", 7, 3, 1)), ("tp", ("gu", ("s", 0), 21, 1))]
                if mli >= 3:
                    w.append(("tp", ("wo", *maw_hist[mli - 3])))
                A("S", f_maw(), waits=w, inc=("qx", 16, ("maw", n, half, jj)))

                def f_wo(n=n, half=half, jj=jj, mli=mli):
                    def f(te):
                        for ti in range(4):
                            tile = half * 4 + ti
                            m = te.matmul(
                                EPm[ti][:, :],
                                maw[mli % 3][:, tile * 128:(tile + 1) * 128],
                                wow[:, jj * 512:(jj + 1) * 512],
                                start=False, stop=(jj == 15),
                                skip_group_check=True)
                        return m
                    return f
                w = [("qx", ("maw", n, half, jj)), ("qw", ("wow", n)),
                     ("tp", ("sd", n, half))]
                A("T", f_wo(), waits=w, inc=("tp", 1, ("wo", n, half, jj)))

            for ti in range(4):
                def f_swev(n=n, half=half, ti=ti):
                    def f(v):
                        tile = half * 4 + ti
                        sl = acc[:, tile * D + n * 512: tile * D + n * 512 + 512]
                        return v.tensor_tensor(sl, sl, EPm[ti][:, :], ALU.add)
                    return f
                A("V", f_swev(), waits=[("tp", ("wo", n, half, 15))],
                  inc=("vp", 1, ("swev", n, half, ti)))

    def f_out(e):
        return dma(e, out_d.ap().rearrange("(t p) f -> p t f", p=128),
                   acc[:, :].rearrange("p (t f) -> p t f", t=NTO))
    A("S", f_out, waits=[("vp", ("swev", 3, 1, 3))], inc=("sto", 16, ("out",)))

    A("S", lambda e: e.nop(), waits=[("sto", ("out",))], inc=None)

    rot = {"qx": 0, "qw": 0}
    events2 = []
    for eng, fn, waits, inc in ev.events:
        if inc is not None and inc[0] in rot:
            sem, amt, key = inc
            phys = f"{sem}{rot[sem] % NROT}"
            rot[sem] += 1
            inc = (phys, amt, key)
        events2.append((eng, fn, waits, inc))
    ev.events = events2

    cnt = defaultdict(int)
    marks = {}
    for eng, fn, waits, inc in ev.events:
        if inc is not None:
            sem, amt, key = inc
            cnt[sem] += amt
            if key is not None:
                assert key not in marks, key
                marks[key] = (sem, cnt[sem])

    def resolve(w):
        sem, key = w
        if isinstance(key, int):
            return sem, key
        msem, val = marks[key]
        return msem, val

    with nc.Block() as block:
        def emit_for(eng_name, selector):
            def body(e):
                hw = defaultdict(int)
                emitted = defaultdict(int)
                for eng, fn, waits, inc in ev.events:
                    if eng != eng_name:
                        if inc is not None and inc[0][:2] in ("qx", "qw"):
                            emitted[inc[0]] += inc[1]
                        continue
                    for w_ in waits:
                        sem, val = resolve(w_)
                        if val > hw[sem]:
                            e.wait_ge(semh[sem], val)
                            hw[sem] = val
                    if inc is not None and inc[0][:2] in ("qx", "qw"):
                        sem = inc[0]
                        prev = emitted[sem]
                        if prev > 0 and prev > hw[sem]:
                            e.wait_ge(semh[sem], prev)
                            hw[sem] = prev
                        emitted[sem] += inc[1]
                    inst = fn(e)
                    if inc is not None:
                        sem, amt, _ = inc
                        inst.then_inc(semh[sem], amt)
            selector(body)

        emit_for("S", block.sync)
        emit_for("G", block.gpsimd)
        emit_for("T", block.tensor)
        emit_for("A", block.scalar)
        emit_for("V", block.vector)

    sw_sc.close()
    sh_sc.close()
    p2b.close()
    root.close()
    return nc


def _rope_tables():
    exps = np.arange(0, 128, 2, dtype=np.float32) / np.float32(128.0)
    inv_freq = (np.float32(1.0) / (np.float32(10000.0) ** exps)).astype(np.float32)
    pos = np.arange(S, dtype=np.float32)
    ang = (pos[:, None] * inv_freq[None, :]).astype(np.float32)
    cos = np.cos(ang.astype(np.float64)).astype(np.float32)
    sin = np.sin(ang.astype(np.float64)).astype(np.float32)
    c1 = np.concatenate([cos, cos], axis=1)
    s1 = np.concatenate([-sin, sin], axis=1)
    C2 = np.ascontiguousarray(np.concatenate([c1, c1], axis=1))
    S2 = np.ascontiguousarray(np.concatenate([s1, s1], axis=1))
    return C2, S2


def _mask_tiles():
    kk = np.arange(128)[:, None]
    qq_ = np.arange(512)[None, :]
    tiles = [(128 * m + kk <= qq_).astype(np.float32) for m in range(4)]
    return np.ascontiguousarray(np.concatenate(tiles, axis=1))


def kernel(L0_attn_x, L0_attn_wq, L0_attn_wk, L0_attn_wv, L0_attn_wo,
           L0_ffn_gate_w, L0_ffn_expert_gate, L0_ffn_expert_up,
           L0_ffn_expert_down, L0_ffn_shared_gate, L0_ffn_shared_up,
           L0_ffn_shared_down):
    x = np.ascontiguousarray(np.asarray(L0_attn_x, dtype=np.float32).reshape(S, D))
    wq = np.asarray(L0_attn_wq, dtype=np.float32)
    wk = np.asarray(L0_attn_wk, dtype=np.float32)
    wv = np.asarray(L0_attn_wv, dtype=np.float32)
    wo = np.ascontiguousarray(np.asarray(L0_attn_wo, dtype=np.float32))
    gate_w = np.asarray(L0_ffn_gate_w, dtype=np.float32)
    eg = np.asarray(L0_ffn_expert_gate, dtype=np.float32)
    eu = np.asarray(L0_ffn_expert_up, dtype=np.float32)
    edn = np.asarray(L0_ffn_expert_down, dtype=np.float32)
    sg = np.asarray(L0_ffn_shared_gate, dtype=np.float32)
    su = np.asarray(L0_ffn_shared_up, dtype=np.float32)
    sdn = np.asarray(L0_ffn_shared_down, dtype=np.float32)

    C2, S2 = _rope_tables()
    shared = dict(
        x_full=x,
        x_t=np.ascontiguousarray(x.T),
        wo_full=wo,
        c2_t=C2, s2_t=S2,
        ident_in=np.eye(128, dtype=np.float32),
        mask_in=_mask_tiles(),
        gate_wTd=np.ascontiguousarray(gate_w.T),
        eg_w=np.ascontiguousarray(np.transpose(eg, (0, 2, 1))),
        eu_w=np.ascontiguousarray(np.transpose(eu, (0, 2, 1))),
        ed_w=np.ascontiguousarray(np.transpose(edn, (0, 2, 1))),
        sg_w=np.ascontiguousarray(sg.T),
        su_w=np.ascontiguousarray(su.T),
        sd_w=np.ascontiguousarray(sdn.T),
    )
    in_maps = []
    for c in range(NCORE):
        m = dict(shared)
        m["x_own"] = np.ascontiguousarray(x[c * TOK:(c + 1) * TOK, :])
        m["wq_s"] = np.ascontiguousarray(wq[:, c * 256:(c + 1) * 256])
        m["wk_s"] = np.ascontiguousarray(wk[:, c * 256:(c + 1) * 256])
        m["wv_s"] = np.ascontiguousarray(wv[:, c * 256:(c + 1) * 256])
        in_maps.append(m)

    nc = _build()
    res = run_bass_kernel_spmd(nc, in_maps, core_ids=list(range(NCORE)))
    out = np.concatenate([res.results[c]["out"] for c in range(NCORE)], axis=0)
    return out.reshape(1, S, D)
